# version 06
# baseline (speedup 1.0000x reference)
import numpy as np
import concourse.bass as bass
import concourse.mybir as mybir
from concourse.bass_utils import run_bass_kernel_spmd

F32 = mybir.dt.float32
F16 = mybir.dt.float16
U8 = mybir.dt.uint8
AT = mybir.ActivationFunctionType
OP = mybir.AluOpType

H = 4096
I = 11008
O = 4096
NCORE = 8
TOK = 1024
CH = 2
CT = 512
NT = 4
KT = H // 128
IT = I // 128
WINS = [(w, min(512, I - w)) for w in range(0, I, 512)]
NW = len(WINS)
NWV = O // 512
SUB = 8
ITERS = 18
SEL_L = 4608
SEL_R = I - SEL_L
K_TOP = I // 2
K_EFF = float(K_TOP) - SEL_R / 2.0
EPS = 1e-5
RING = 3
SLOTB = 4096


def _subs(nk):
    out, k = [], 0
    while k < nk:
        c = min(SUB, nk - k)
        out.append((k, c))
        k += c
    return out


SUBS_KT = _subs(KT)
SUBS_IT = _subs(IT)

ITEMS = []
for _w, _wl in WINS:
    for _k0, _nk in SUBS_KT:
        ITEMS.append(("g", (_w, _wl, _k0, _nk)))
ITEMS.append(("z", ()))
for _k0, _nk in SUBS_KT:
    ITEMS.append(("a", (_k0, _nk)))
for _wv in range(NWV):
    for _k0, _nk in SUBS_KT:
        ITEMS.append(("b", (_wv, _k0, _nk)))
    ITEMS.append(("bb", (_wv,)))
    ITEMS.append(("bw", (_wv,)))
for _w, _wl in WINS:
    for _k0, _nk in SUBS_KT:
        ITEMS.append(("u", (_w, _wl, _k0, _nk)))
for _hw in range(NWV):
    for _k0, _nk in SUBS_IT:
        ITEMS.append(("d", (_hw, _k0, _nk)))
NITEM = len(ITEMS)

NPT = 128 + NT + NT * IT


def build_program():
    nc = bass.Bass("TRN2", target_bir_lowering=False, debug=False)

    x_h = nc.dram_tensor("x", [TOK, H], F32, kind="ExternalInput").ap()
    wgT_h = nc.dram_tensor("wgT", [H, I], F32, kind="ExternalInput").ap()
    wuT_h = nc.dram_tensor("wuT", [H, I], F32, kind="ExternalInput").ap()
    wdT_h = nc.dram_tensor("wdT", [I, O], F32, kind="ExternalInput").ap()
    wbT_h = nc.dram_tensor("wbT", [H, O], F32, kind="ExternalInput").ap()
    bb_h = nc.dram_tensor("bb", [1, O], F32, kind="ExternalInput").ap()
    wz_h = nc.dram_tensor("wzthrT", [H, 9], F32, kind="ExternalInput").ap()
    bthr_h = nc.dram_tensor("bthr", [128, 1], F32, kind="ExternalInput").ap()
    awT_h = nc.dram_tensor("awT", [H, 128], F32, kind="ExternalInput").ap()
    bwT_h = nc.dram_tensor("bwT", [128, O], F32, kind="ExternalInput").ap()
    eye_h = nc.dram_tensor("eye", [128, 128], F32, kind="ExternalInput").ap()
    y_h = nc.dram_tensor("y", [TOK, H], F32, kind="ExternalOutput").ap()
    ffns_h = nc.dram_tensor("ffn_scr", [NT, NWV, 128, 512], F16,
                            kind="Internal").ap()

    def item_src(kind, meta):
        if kind in ("g", "u"):
            w, wl, k0, nk = meta
            hb = wgT_h if kind == "g" else wuT_h
            return hb[k0 * 128:(k0 + nk) * 128, w:w + wl].rearrange(
                "(t p) n -> p t n", p=128), nk, wl, False
        if kind == "b":
            wv, k0, nk = meta
            return wbT_h[k0 * 128:(k0 + nk) * 128,
                         wv * 512:(wv + 1) * 512].rearrange(
                "(t p) n -> p t n", p=128), nk, 512, False
        if kind == "d":
            hw, k0, nk = meta
            return wdT_h[k0 * 128:(k0 + nk) * 128,
                         hw * 512:(hw + 1) * 512].rearrange(
                "(t p) n -> p t n", p=128), nk, 512, False
        if kind == "z":
            return wz_h.rearrange("(t p) n -> p t n", p=128), KT, 9, False
        if kind == "a":
            k0, nk = meta
            return awT_h[k0 * 128:(k0 + nk) * 128, :].rearrange(
                "(t p) n -> p t n", p=128), nk, 128, False
        if kind == "bw":
            wv, = meta
            return bwT_h[:, wv * 512:(wv + 1) * 512], 1, 512, False
        if kind == "bb":
            wv, = meta
            return bb_h[:, wv * 512:(wv + 1) * 512], 1, 512, True
        raise AssertionError(kind)

    def n_pG(c, wi, m):
        return c * (NW * NT) + wi * NT + m + 1

    def n_pB(c, wv, m):
        return c * (NWV * 3) + wv * 3 + max(m, 1)

    def n_pU(c, wi, m):
        return c * (NW * NT) + wi * NT + m + 1

    def n_pD(c, hw, m):
        return c * (NWV * NT) + hw * NT + m + 1

    def n_vPC(c, wi, m):
        return c * (NW * NT) + wi * NT + m + 1

    def n_vMID(c, m, it):
        return c * NT * (ITERS + 1) + m * (ITERS + 1) + it + 1

    def n_vMK(c, m):
        return c * NT + m + 1

    def n_vXN(c, m):
        return c * NT + m + 1

    def n_vVAR(c, m):
        return c * NT + m + 1

    def n_aRS(c, m):
        return c * NT + m + 1

    def n_aSQ(c, m):
        return c * (2 * NT) + 2 * m + 2

    def n_aGE(c, wi, m):
        return c * (NW * NT) + wi * NT + m + 1

    def n_aCR(c, m, it):
        return c * NT * ITERS + m * ITERS + it + 1

    def n_aFE(c, wv, m):
        return c * (NWV * NT) + wv * NT + m + 1

    def n_aPT(i):
        return i + 1

    def pt_idx(c, phase, a, b=0):
        base = c * NPT
        if phase == "ln":
            return base + a * KT + b
        if phase == "hw":
            return base + 128 + a
        return base + 128 + NT + a * IT + b

    def n_vEP(c, hw, m):
        return c * (NWV * NT) + hw * NT + m + 1

    from contextlib import ExitStack
    with ExitStack() as ctx:
        en = ctx.enter_context

        def sbuf(name, shape, dt):
            return en(nc.sbuf_tensor(name, shape, dt))

        S = sbuf("S", [128, NT * I], F16)
        xT = sbuf("xT", [128, KT * 512], F16)
        sc27 = sbuf("sc27", [128, I], F16)
        ovr = sbuf("ovr", [128, I], F16)
        wst = sbuf("wst", [128, RING * SLOTB], F16)
        xn16 = sbuf("xn16", [128, H], F16)
        ffst = sbuf("ffst", [128, 4 * 512], F16)
        xs = sbuf("xs", [128, 2 * 512], F32)
        eye16 = sbuf("eye16", [128, 128], F16)
        ones16 = sbuf("ones16", [1, 128], F16)
        h16 = sbuf("h16", [128, NT * 128], F16)
        hwT16 = sbuf("hwT16", [128, NT * 128], F16)
        g9f = sbuf("g9f", [128, NT * 9], F32)
        e8 = sbuf("e8", [128, NT * 8], F32)
        w8 = sbuf("w8", [128, NT * 8], F32)
        wn = sbuf("wn", [128, NT * 8], F32)
        mx4 = sbuf("mx4", [128, NT], F32)
        nmx4 = sbuf("nmx4", [128, NT], F32)
        se4 = sbuf("se4", [128, NT], F32)
        rse4 = sbuf("rse4", [128, NT], F32)
        th4 = sbuf("th4", [128, NT], F32)
        thv4 = sbuf("thv4", [128, NT], F32)
        ws4 = sbuf("ws4", [128, NT], F32)
        wz4 = sbuf("wz4", [128, NT], F32)
        wr2 = sbuf("wr2", [128, NT], F32)
        bthr2 = sbuf("bthr2", [128, 1], F32)
        bthr2h = sbuf("bthr2h", [128, 1], F32)
        st = sbuf("st", [128, 3 * NT], F32)
        cl = sbuf("cl", [128, 2], F32)
        cu = sbuf("cu", [128, 1], F32)
        pred = sbuf("pred", [128, 2], U8)
        lns = sbuf("lns", [128, 52 * NT], F32)

        sab = ovr[:, 0:I]
        ovr32 = ovr.bitcast(F32)
        xbuf = ovr32[:, 0:H]
        out32 = [ovr32[:, H + p * 512:H + (p + 1) * 512] for p in range(2)]
        sc32 = sc27.bitcast(F32)
        junkL = sc27[:, 0:SEL_L]
        junkR = sc27[:, SEL_L:I]

        quad = [en(nc.psum_tensor(f"q{i}", [128, 512], F32)) for i in range(4)]
        pt16 = [en(nc.psum_tensor(f"pt{i}", [128, 128], F16)) for i in range(4)]
        NPTB = 4

        def sem(name):
            return en(nc.semaphore(name))

        dX = sem("dX"); dCON = sem("dCON")
        dW = [sem(f"dW{i}") for i in range(RING)]
        dXS = [sem("dXS0"), sem("dXS1")]
        dFW = [sem("dFW0"), sem("dFW1")]
        dFR = [sem("dFR0"), sem("dFR1")]
        dY = [sem("dY0"), sem("dY1")]
        pLT = sem("pLT"); pG = sem("pG"); pGH = sem("pGH"); pHT = sem("pHT")
        pB = sem("pB"); pU = sem("pU"); pST = sem("pST"); pD = sem("pD")
        pWF = sem("pWF")
        aPT = sem("aPT"); aGE = sem("aGE"); aCR = sem("aCR")
        aEXP = sem("aEXP"); aHWT = sem("aHWT"); aFE = sem("aFE")
        aSE = sem("aSE"); aSQ = sem("aSQ"); aRS = sem("aRS")
        vXN = sem("vXN"); vVAR = sem("vVAR"); vG9 = sem("vG9")
        vNMX = sem("vNMX"); vGA = sem("vGA")
        vMID = sem("vMID"); vMK = sem("vMK"); vPC = sem("vPC"); vEP = sem("vEP")
        cD = sem("cD"); cA = sem("cA")

        block = en(nc.Block())

        @block.sync
        def _(sync):
            for c in range(CH):
                for m in range(NT):
                    if c * NT + m > 0:
                        sync.wait_ge(vXN, c * NT + m)
                    if m == 0 and c > 0:
                        sync.wait_ge(vEP, c * NWV * NT)
                    r0 = c * CT + m * 128
                    sync.dma_start(out=xbuf, in_=x_h[r0:r0 + 128, :]
                                   ).then_inc(dX, 16)
                for wv in range(NWV):
                    for m in range(NT):
                        j = wv * NT + m
                        p = j % 2
                        sync.wait_ge(aFE, c * 32 + j + 1)
                        sync.dma_start(out=ffns_h[m, wv, :, :],
                                       in_=ffst[:, p * 512:(p + 1) * 512]
                                       ).then_inc(dFW[p], 16)
                sync.wait_ge(vMK, c * NT + NT)

                def ep_read(j):
                    hw, m = divmod(j, NT)
                    p = j % 2
                    if c * 32 + j - 2 >= 0 and j >= 2:
                        sync.wait_ge(vEP, c * 32 + j - 1)
                    sync.wait_ge(dFW[p], 16 * (c * 16 + j // 2 + 1))
                    sync.dma_start(
                        out=ffst[:, (2 + p) * 512:(3 + p) * 512],
                        in_=ffns_h[m, hw, :, :]).then_inc(dFR[p], 16)
                    r0 = c * CT + m * 128
                    sync.dma_start(
                        out=xs[:, p * 512:(p + 1) * 512],
                        in_=x_h[r0:r0 + 128, hw * 512:(hw + 1) * 512]
                    ).then_inc(dXS[p], 16)

                def ep_store(j):
                    hw, m = divmod(j, NT)
                    p = j % 2
                    sync.wait_ge(vEP, c * 32 + j + 1)
                    r0 = c * CT + m * 128
                    sync.dma_start(
                        out=y_h[r0:r0 + 128, hw * 512:(hw + 1) * 512],
                        in_=out32[p]).then_inc(dY[p], 16)

                ep_read(0)
                ep_read(1)
                for j in range(NWV * NT):
                    ep_store(j)
                    if j + 2 < NWV * NT:
                        ep_read(j + 2)
            sync.wait_ge(dY[0], 16 * (CH * 16))
            sync.wait_ge(dY[1], 16 * (CH * 16))

        @block.gpsimd
        def _(gpsimd):
            gpsimd.dma_start(out=eye16[:, :], in_=eye_h).then_inc(dCON, 16)
            gpsimd.dma_start(out=bthr2[:, :], in_=bthr_h).then_inc(dCON, 16)
            for c in range(CH):
                for j, (kind, meta) in enumerate(ITEMS):
                    gj = c * NITEM + j
                    slot = gj % RING
                    if gj - (RING - 1) > 0:
                        gpsimd.wait_ge(pWF, gj - (RING - 1))
                    src, t, n, is_1p = item_src(kind, meta)
                    if kind in ("bw",):
                        dst = wst[:, slot * SLOTB:slot * SLOTB + 512]
                    elif kind == "bb":
                        dst = wst[0:1, slot * SLOTB:slot * SLOTB + 512]
                    else:
                        dst = wst[:, slot * SLOTB:slot * SLOTB + t * n
                                  ].rearrange("p (t n) -> p t n", t=t)
                    gpsimd.dma_start(out=dst, in_=src).then_inc(dW[slot], 16)

        @block.tensor
        def _(tensor):
            tensor.wait_ge(dCON, 32)
            fpend = [0]
            gseen = [0]

            def wait_item(gj):
                slot = gj % RING
                tensor.wait_ge(dW[slot], 16 * (gj // RING + 1))
                if gseen[0] > 0:
                    fpend[0] += 1
                gseen[0] += 1
                return slot

            def mark(mm):
                if fpend[0] > 0 and mm is not None:
                    mm.then_inc(pWF, fpend[0])
                    fpend[0] = 0
                return mm

            def pt_wait(i):
                if i >= NPTB:
                    tensor.wait_ge(aPT, i - (NPTB - 1))

            for c in range(CH):
                for m in range(NT):
                    tensor.wait_ge(vXN, n_vXN(c, m))
                    for k in range(KT):
                        i = pt_idx(c, "ln", m, k)
                        pt_wait(i)
                        tensor.transpose(pt16[i % NPTB][:, :],
                                         xn16[:, k * 128:(k + 1) * 128],
                                         eye16[:, :]).then_inc(pLT, 1)
                gj = c * NITEM
                for wi, (w, wl) in enumerate(WINS):
                    for k0, nk in SUBS_KT:
                        slot = wait_item(gj)
                        if wi == 0 and k0 == 0:
                            tensor.wait_ge(aPT, c * NPT + 128)
                        for kk in range(nk):
                            k = k0 + kk
                            for m in range(NT):
                                if k == 0 and wi > 0:
                                    tensor.wait_ge(aGE, n_aGE(c, wi - 1, m))
                                elif k == 0 and c > 0:
                                    tensor.wait_ge(vEP,
                                                   n_vEP(c - 1, NWV - 1, m))
                                mm = mark(tensor.matmul(
                                    quad[m][:, 0:wl],
                                    xT[:, k * 512 + m * 128:
                                       k * 512 + (m + 1) * 128],
                                    wst[:, slot * SLOTB + kk * wl:
                                        slot * SLOTB + (kk + 1) * wl],
                                    start=(k == 0), stop=(k == KT - 1)))
                                mm.annotate("mm_gate")
                                if k == KT - 1:
                                    mm.then_inc(pG, 1)
                        gj += 1
                slot = wait_item(gj)
                for k in range(KT):
                    for m in range(NT):
                        if k == 0:
                            tensor.wait_ge(aGE, n_aGE(c, NW - 1, m))
                        mm = mark(tensor.matmul(
                            quad[m][:, 0:9],
                            xT[:, k * 512 + m * 128:k * 512 + (m + 1) * 128],
                            wst[:, slot * SLOTB + k * 9:
                                slot * SLOTB + (k + 1) * 9],
                            start=(k == 0), stop=(k == KT - 1)))
                mm.then_inc(pGH, 1)
                gj += 1
                for k0, nk in SUBS_KT:
                    slot = wait_item(gj)
                    for kk in range(nk):
                        k = k0 + kk
                        for m in range(NT):
                            if k == 0:
                                tensor.wait_ge(vG9, c * 8 + m + 1)
                            mm = mark(tensor.matmul(
                                quad[m][:, 0:128],
                                xT[:, k * 512 + m * 128:
                                   k * 512 + (m + 1) * 128],
                                wst[:, slot * SLOTB + kk * 128:
                                    slot * SLOTB + (kk + 1) * 128],
                                start=(k == 0), stop=(k == KT - 1)))
                    gj += 1
                mm.then_inc(pGH, 1)
                tensor.wait_ge(vGA, c + 1)
                for m in range(NT):
                    i = pt_idx(c, "hw", m)
                    pt_wait(i)
                    tensor.transpose(pt16[i % NPTB][:, :],
                                     h16[:, m * 128:(m + 1) * 128],
                                     eye16[:, :]).then_inc(pHT, 1)
                for wv in range(NWV):
                    for k0, nk in SUBS_KT:
                        slot = wait_item(gj)
                        for kk in range(nk):
                            k = k0 + kk
                            for m in range(NT):
                                if k == 0:
                                    if wv > 0:
                                        tensor.wait_ge(aFE, n_aFE(c, wv - 1, m))
                                    else:
                                        tensor.wait_ge(vG9, c * 8 + 4 + m + 1)
                                mm = mark(tensor.matmul(
                                    quad[m][:, 0:512],
                                    xT[:, k * 512 + m * 128:
                                       k * 512 + (m + 1) * 128],
                                    wst[:, slot * SLOTB + kk * 512:
                                        slot * SLOTB + (kk + 1) * 512],
                                    start=(k == 0), stop=False))
                                mm.annotate("mm_base")
                        gj += 1
                    slot_bb = wait_item(gj)
                    gj += 1
                    if wv == 0:
                        tensor.wait_ge(aPT, c * NPT + 128 + NT)
                    for m in range(NT):
                        mark(tensor.matmul(quad[m][:, 0:512], ones16[:, :],
                                           wst[0:1, slot_bb * SLOTB:
                                               slot_bb * SLOTB + 512],
                                           start=False, stop=False))
                    slot_bw = wait_item(gj)
                    gj += 1
                    for m in range(NT):
                        mm = tensor.matmul(
                            quad[m][:, 0:512],
                            hwT16[:, m * 128:(m + 1) * 128],
                            wst[:, slot_bw * SLOTB:slot_bw * SLOTB + 512],
                            start=False, stop=True)
                        if m == 0:
                            mark(mm)
                        else:
                            mm.then_inc(pB, 1)
                for wi, (w, wl) in enumerate(WINS):
                    for k0, nk in SUBS_KT:
                        slot = wait_item(gj)
                        for kk in range(nk):
                            k = k0 + kk
                            for m in range(NT):
                                if k == 0:
                                    if wi > 0:
                                        tensor.wait_ge(vPC, n_vPC(c, wi - 1, m))
                                    else:
                                        tensor.wait_ge(aFE,
                                                       n_aFE(c, NWV - 1, m))
                                mm = mark(tensor.matmul(
                                    quad[m][:, 0:wl],
                                    xT[:, k * 512 + m * 128:
                                       k * 512 + (m + 1) * 128],
                                    wst[:, slot * SLOTB + kk * wl:
                                        slot * SLOTB + (kk + 1) * wl],
                                    start=(k == 0), stop=(k == KT - 1)))
                                mm.annotate("mm_up")
                                if k == KT - 1:
                                    mm.then_inc(pU, 1)
                        gj += 1
                for m in range(NT):
                    tensor.wait_ge(vPC, n_vPC(c, NW - 1, m))
                    for kt in range(IT):
                        i = pt_idx(c, "su", m, kt)
                        pt_wait(i)
                        tensor.transpose(
                            pt16[i % NPTB][:, :],
                            S[:, m * I + kt * 128:m * I + (kt + 1) * 128],
                            eye16[:, :]).then_inc(pST, 1)
                for hw in range(NWV):
                    for k0, nk in SUBS_IT:
                        slot = wait_item(gj)
                        for kk in range(nk):
                            k = k0 + kk
                            for m in range(NT):
                                if k == 0:
                                    tensor.wait_ge(
                                        aPT,
                                        c * NPT + 128 + NT + (m + 1) * IT)
                                    if hw > 0:
                                        tensor.wait_ge(vEP, n_vEP(c, hw - 1, m))
                                if m == 0:
                                    lhs = sc27[:, k * 128:(k + 1) * 128]
                                else:
                                    lhs = S[:, (m - 1) * I + k * 128:
                                            (m - 1) * I + (k + 1) * 128]
                                mm = mark(tensor.matmul(
                                    quad[m][:, 0:512], lhs,
                                    wst[:, slot * SLOTB + kk * 512:
                                        slot * SLOTB + (kk + 1) * 512],
                                    start=(k == 0), stop=(k == IT - 1)))
                                mm.annotate("mm_down")
                                if k == IT - 1:
                                    mm.then_inc(pD, 1)
                        gj += 1
                assert gj == (c + 1) * NITEM, (gj, NITEM)

        @block.scalar
        def _(scalar):
            napt = [0]

            for c in range(CH):
                scalar.preload_act_table(AT.Sqrt)
                for m in range(NT):
                    scalar.wait_ge(vVAR, n_vVAR(c, m))
                    scalar.activation(out=lns[:, 52 * m + 50:52 * m + 51],
                                      in_=lns[:, 52 * m + 49:52 * m + 50],
                                      func=AT.Sqrt).then_inc(aRS, 1)
                    for k in range(KT):
                        i = pt_idx(c, "ln", m, k)
                        scalar.wait_ge(pLT, c * 128 + m * KT + k + 1)
                        scalar.copy(
                            out=xT[:, k * 512 + m * 128:
                                   k * 512 + (m + 1) * 128],
                            in_=pt16[i % NPTB][:, :]).then_inc(aPT, 1)
                        napt[0] += 1
                scalar.preload_act_table(AT.Silu)
                for wi, (w, wl) in enumerate(WINS):
                    for m in range(NT):
                        scalar.wait_ge(pG, n_pG(c, wi, m))
                        scalar.activation(
                            out=S[:, m * I + w:m * I + w + wl],
                            in_=quad[m][:, 0:wl],
                            func=AT.Silu).then_inc(aGE, 1)
                scalar.preload_act_table(AT.Exp)
                scalar.wait_ge(vNMX, c + 1)
                for m in range(NT):
                    scalar.activation(
                        out=e8[:, m * 8:(m + 1) * 8],
                        in_=g9f[:, m * 9:m * 9 + 8],
                        func=AT.Exp, bias=nmx4[:, m:m + 1],
                        accum_out=se4[:, m:m + 1]).then_inc(aEXP, 1)
                scalar.activation(out=th4[:, :], in_=g9f[:, 8::9], func=AT.Tanh,
                                  bias=bthr2h[:, :], scale=0.5
                                  ).then_inc(aEXP, 1)
                scalar.activation(out=wz4[:, :], in_=g9f[:, 0:NT],
                                  func=AT.Identity
                                  ).then_inc(aEXP, 1)
                pend_hwt = list(range(NT))
                pend_ffn = [(wv, m) for wv in range(NWV) for m in range(NT)]

                def drain_hwt(mm):
                    i = pt_idx(c, "hw", mm)
                    scalar.wait_ge(pHT, c * NT + mm + 1)
                    scalar.copy(out=hwT16[:, mm * 128:(mm + 1) * 128],
                                in_=pt16[i % NPTB][:, :]).then_inc(aPT, 1)
                    napt[0] += 1

                def drain_ffn():
                    wv, m = pend_ffn.pop(0)
                    j = wv * NT + m
                    p = j % 2
                    scalar.wait_ge(pB, n_pB(c, wv, m))
                    iu = c * 16 + j // 2
                    if iu > 0:
                        scalar.wait_ge(dFW[p], 16 * iu)
                    scalar.copy(out=ffst[:, p * 512:(p + 1) * 512],
                                in_=quad[m][:, 0:512]).then_inc(aFE, 1)

                for m in range(NT):
                    if c * NT + m > 0:
                        scalar.wait_ge(vMK, c * NT + m)
                    if m == 0 and c > 0:
                        scalar.wait_ge(dY[0], 16 * (c * 16))
                        scalar.wait_ge(dY[1], 16 * (c * 16))
                    scalar.activation(out=sab, in_=S[:, m * I:(m + 1) * I],
                                      func=AT.Abs).then_inc(cA, 1)
                    for it in range(ITERS):
                        scalar.wait_ge(vMID, n_vMID(c, m, it))
                        if it == 0:
                            scalar.wait_ge(cA, c * NT + m + 1)
                        scalar.activation(
                            out=junkR, in_=sab[:, SEL_L:I],
                            func=AT.Sign, bias=st[:, 3 * m + 2:3 * m + 3],
                            scale=-1.0,
                            accum_out=cl[:, 1:2]).annotate("selR"
                            ).then_inc(aCR, 1)
                        if m == 0 and it < NT:
                            drain_hwt(pend_hwt.pop(0))
                        elif pend_ffn and (m >= 1 or it >= 4):
                            drain_ffn()
                            if pend_ffn:
                                drain_ffn()
                while pend_ffn:
                    drain_ffn()
                for m in range(NT):
                    for kt in range(IT):
                        i = pt_idx(c, "su", m, kt)
                        scalar.wait_ge(pST, c * NT * IT + m * IT + kt + 1)
                        if m == 0:
                            dst = sc27[:, kt * 128:(kt + 1) * 128]
                        else:
                            dst = S[:, (m - 1) * I + kt * 128:
                                    (m - 1) * I + (kt + 1) * 128]
                        scalar.copy(out=dst, in_=pt16[i % NPTB][:, :]
                                    ).then_inc(aPT, 1)
                        napt[0] += 1

        @block.vector
        def _(vector):
            nD = [0]
            last = [None]

            def di(ins):
                nD[0] += 1
                ins.then_inc(cD, 1)
                last[0] = ins
                return nD[0]

            def dc(ins, sem_, k=1):
                ins.then_inc(sem_, k)
                last[0] = ins
                return ins

            def dw(n):
                vector.wait_ge(cD, n)

            vector.memset(ones16[:, :], 1.0)
            vector.wait_ge(dCON, 32)
            n = di(vector.tensor_scalar(out=bthr2h[:, :], in0=bthr2[:, :], scalar1=0.5,
                                        scalar2=None, op0=OP.mult))
            for c in range(CH):
                for m in range(NT):
                    vector.wait_ge(dX, 16 * (c * NT + m + 1))
                    b0 = 52 * m
                    bn48 = lns[:, b0:b0 + 48]
                    mv = lns[:, b0 + 48:b0 + 50]
                    mean = lns[:, b0 + 48:b0 + 49]
                    var = lns[:, b0 + 49:b0 + 50]
                    sd = lns[:, b0 + 50:b0 + 51]
                    tmp = lns[:, b0 + 51:b0 + 52]
                    for q in range(8):
                        n = di(vector.bn_stats(lns[:, b0 + 6 * q:b0 + 6 * q + 6],
                                               xbuf[:, 512 * q:512 * (q + 1)]))
                    dw(n)
                    n = di(vector.bn_aggr(mv, bn48))
                    dw(n)
                    dc(vector.tensor_scalar(out=var, in0=var,
                                             scalar1=float(EPS),
                                             scalar2=None, op0=OP.add), vVAR)
                    vector.wait_ge(aRS, n_aRS(c, m))
                    n = di(vector.reciprocal(tmp, sd))
                    dw(n)
                    n = di(vector.tensor_tensor(out=sd, in0=tmp, in1=tmp,
                                                op=OP.mult))
                    dw(n)
                    n = di(vector.tensor_tensor(out=sd, in0=sd, in1=var,
                                                op=OP.mult))
                    dw(n)
                    n = di(vector.tensor_scalar(out=sd, in0=sd, scalar1=-0.5,
                                                scalar2=1.5, op0=OP.mult,
                                                op1=OP.add))
                    dw(n)
                    n = di(vector.tensor_tensor(out=sd, in0=tmp, in1=sd,
                                                op=OP.mult))
                    dw(n)
                    vector.wait_ge(pLT, c * 128 + m * KT)
                    dc(vector.tensor_scalar(out=xn16[:, :], in0=xbuf,
                                             scalar1=mean, scalar2=sd,
                                             op0=OP.subtract, op1=OP.mult),
                       vXN)
                vector.wait_ge(pGH, c * 2 + 1)
                for m in range(NT):
                    dc(vector.tensor_copy(g9f[:, m * 9:(m + 1) * 9],
                                          quad[m][:, 0:9]), vG9)
                vector.wait_ge(pGH, c * 2 + 2)
                for m in range(NT):
                    dc(vector.tensor_copy(h16[:, m * 128:(m + 1) * 128],
                                          quad[m][:, 0:128]), vG9)
                vector.wait_ge(vG9, c * 8 + 8)
                for m in range(NT):
                    n = di(vector.reduce_max(mx4[:, m:m + 1],
                                             g9f[:, m * 9:m * 9 + 8],
                                             axis=mybir.AxisListType.X))
                dw(n)
                dc(vector.tensor_scalar(out=nmx4[:, :], in0=mx4[:, :],
                                        scalar1=-1.0, scalar2=None,
                                        op0=OP.mult), vNMX)
                vector.wait_ge(aEXP, c * 6 + 6)
                n = di(vector.reciprocal(rse4[:, :], se4[:, :]))
                n = di(vector.tensor_scalar(out=thv4[:, :], in0=th4[:, :], scalar1=0.0625,
                                            scalar2=0.0625, op0=OP.mult,
                                            op1=OP.add))
                dw(n)
                for m in range(NT):
                    n = di(vector.tensor_scalar(
                        out=w8[:, m * 8:(m + 1) * 8],
                        in0=e8[:, m * 8:(m + 1) * 8],
                        scalar1=rse4[:, m:m + 1], scalar2=None, op0=OP.mult))
                dw(n)
                for m in range(NT):
                    n = di(vector.tensor_scalar(
                        out=w8[:, m * 8:(m + 1) * 8],
                        in0=w8[:, m * 8:(m + 1) * 8],
                        scalar1=thv4[:, m:m + 1], scalar2=0.0,
                        op0=OP.subtract, op1=OP.max))
                dw(n)
                for m in range(NT):
                    n = di(vector.reduce_sum(ws4[:, m:m + 1],
                                             w8[:, m * 8:(m + 1) * 8],
                                             axis=mybir.AxisListType.X))
                dw(n)
                n = di(vector.tensor_scalar(out=wz4[:, :], in0=ws4[:, :], scalar1=0.0,
                                            scalar2=None, op0=OP.is_equal))
                dw(n)
                n = di(vector.tensor_tensor(out=ws4[:, :], in0=ws4[:, :], in1=wz4[:, :],
                                            op=OP.add))
                dw(n)
                n = di(vector.reciprocal(wr2[:, :], ws4[:, :]))
                dw(n)
                n = di(vector.tensor_scalar(out=wr2[:, :], in0=wr2[:, :], scalar1=2.0,
                                            scalar2=None, op0=OP.mult))
                dw(n)
                for m in range(NT):
                    n = di(vector.tensor_scalar(
                        out=wn[:, m * 8:(m + 1) * 8],
                        in0=w8[:, m * 8:(m + 1) * 8],
                        scalar1=wr2[:, m:m + 1], scalar2=None, op0=OP.mult))
                dw(n)
                for m in range(NT):
                    for e in range(8):
                        ins = vector.tensor_scalar(
                            out=h16[:, m * 128 + e * 16:m * 128 + (e + 1) * 16],
                            in0=h16[:, m * 128 + e * 16:m * 128 + (e + 1) * 16],
                            scalar1=wn[:, m * 8 + e:m * 8 + e + 1],
                            scalar2=None, op0=OP.mult)
                        if m == NT - 1 and e == 7:
                            dc(ins, vGA)
                        else:
                            di(ins)
                for m in range(NT):
                    lo = st[:, 3 * m:3 * m + 1]
                    hi = st[:, 3 * m + 1:3 * m + 2]
                    mid = st[:, 3 * m + 2:3 * m + 3]
                    vector.wait_ge(aGE, n_aGE(c, NW - 1, m))
                    n = di(vector.memset(lo, 0.0))
                    n = di(vector.reduce_max(hi, S[:, m * I:(m + 1) * I],
                                             axis=mybir.AxisListType.X,
                                             apply_absolute_value=True))
                    dw(n)
                    n = di(vector.tensor_add(mid, lo, hi))
                    dw(n)
                    dc(vector.tensor_scalar(out=mid, in0=mid, scalar1=0.5,
                                             scalar2=None, op0=OP.mult), vMID)
                    for it in range(ITERS):
                        vector.wait_ge(vMID, n_vMID(c, m, it))
                        if it == 0:
                            vector.wait_ge(aCR, c * NT * ITERS + m * ITERS + 1)
                        ins_cnt = vector.tensor_scalar(
                            out=junkL, in0=sab[:, 0:SEL_L],
                            scalar1=mid, scalar2=0.0,
                            op0=OP.is_le, op1=OP.add,
                            accum_out=cl[:, 0:1])
                        ins_cnt.annotate("selL")
                        n = di(ins_cnt)
                        vector.wait_ge(aCR, n_aCR(c, m, it))
                        dw(n)
                        n = di(vector.scalar_tensor_tensor(
                            out=cu[:, :], in0=cl[:, 1:2], scalar=0.5,
                            in1=cl[:, 0:1], op0=OP.mult, op1=OP.add))
                        dw(n)
                        n = di(vector.tensor_scalar(
                            out=pred[:, 0:1], in0=cu[:, :], scalar1=K_EFF,
                            scalar2=None, op0=OP.is_ge))
                        n = di(vector.tensor_scalar(
                            out=pred[:, 1:2], in0=cu[:, :], scalar1=K_EFF,
                            scalar2=None, op0=OP.is_lt))
                        dw(n)
                        n = di(vector.copy_predicated(hi, pred[:, 0:1], mid))
                        n = di(vector.copy_predicated(lo, pred[:, 1:2], mid))
                        dw(n)
                        n = di(vector.tensor_add(mid, lo, hi))
                        dw(n)
                        dc(vector.tensor_scalar(
                            out=mid, in0=mid, scalar1=0.5, scalar2=None,
                            op0=OP.mult), vMID)
                    n = di(vector.tensor_scalar(
                        out=junkL, in0=sab[:, 0:SEL_L], scalar1=hi,
                        scalar2=None, op0=OP.is_gt))
                    n = di(vector.tensor_scalar(
                        out=junkR, in0=sab[:, SEL_L:I], scalar1=hi,
                        scalar2=None, op0=OP.is_gt))
                    dw(n)
                    di(vector.tensor_tensor(
                        out=S[:, m * I:m * I + SEL_L], in0=junkL,
                        in1=S[:, m * I:m * I + SEL_L], op=OP.mult))
                    dc(vector.tensor_tensor(
                        out=S[:, m * I + SEL_L:(m + 1) * I], in0=junkR,
                        in1=S[:, m * I + SEL_L:(m + 1) * I], op=OP.mult), vMK)
                for wi, (w, wl) in enumerate(WINS):
                    for m in range(NT):
                        if wi == 0:
                            vector.wait_ge(vMK, n_vMK(c, m))
                        vector.wait_ge(pU, n_pU(c, wi, m))
                        dc(vector.tensor_tensor(
                            out=S[:, m * I + w:m * I + w + wl],
                            in0=S[:, m * I + w:m * I + w + wl],
                            in1=quad[m][:, 0:wl], op=OP.mult), vPC)
                for hw in range(NWV):
                    for m in range(NT):
                        j = hw * NT + m
                        p = j % 2
                        vector.wait_ge(pD, n_pD(c, hw, m))
                        vector.wait_ge(dFR[p], 16 * (c * 16 + j // 2 + 1))
                        vector.wait_ge(dXS[p], 16 * (c * 16 + j // 2 + 1))
                        iu = c * 16 + j // 2
                        if iu > 0:
                            vector.wait_ge(dY[p], 16 * iu)
                        n = di(vector.tensor_tensor(
                            out=out32[p], in0=quad[m][:, 0:512],
                            in1=ffst[:, (2 + p) * 512:(3 + p) * 512],
                            op=OP.add))
                        dw(n)
                        dc(vector.tensor_tensor(
                            out=out32[p], in0=out32[p],
                            in1=xs[:, p * 512:(p + 1) * 512], op=OP.add), vEP)

    return nc


def kernel(L0_attn_x, L0_attn_w_gate, L0_attn_w_up, L0_attn_w_down,
           L0_ffn_W_base, L0_ffn_b_base, L0_ffn_W_gate, L0_ffn_W_thr,
           L0_ffn_b_thr, L0_ffn_A_w, L0_ffn_B_w):
    x = np.ascontiguousarray(np.asarray(L0_attn_x, dtype=np.float32)
                             ).reshape(NCORE * TOK, H)
    wgT = np.ascontiguousarray(np.asarray(L0_attn_w_gate, np.float32).T)
    wuT = np.ascontiguousarray(np.asarray(L0_attn_w_up, np.float32).T)
    wdT = np.ascontiguousarray(np.asarray(L0_attn_w_down, np.float32).T)
    wbT = np.ascontiguousarray(np.asarray(L0_ffn_W_base, np.float32).T)
    bb = np.ascontiguousarray(np.asarray(L0_ffn_b_base, np.float32)
                              ).reshape(1, O)
    wz = np.asarray(L0_ffn_W_gate, np.float32).T
    wthr = np.asarray(L0_ffn_W_thr, np.float32).reshape(H, 1)
    wzthrT = np.ascontiguousarray(np.concatenate([wz, wthr], axis=1))
    bthr = np.broadcast_to(np.asarray(L0_ffn_b_thr, np.float32).reshape(1, 1),
                           (128, 1)).copy()
    awT = np.ascontiguousarray(
        np.asarray(L0_ffn_A_w, np.float32).reshape(128, H).T)
    bwT = np.ascontiguousarray(
        np.asarray(L0_ffn_B_w, np.float32).transpose(0, 2, 1).reshape(128, O))
    eye = np.eye(128, dtype=np.float32)

    nc = build_program()
    shared = dict(wgT=wgT, wuT=wuT, wdT=wdT, wbT=wbT, bb=bb, wzthrT=wzthrT,
                  bthr=bthr, awT=awT, bwT=bwT, eye=eye)
    in_maps = []
    for cid in range(NCORE):
        m = dict(shared)
        m["x"] = np.ascontiguousarray(x[cid * TOK:(cid + 1) * TOK])
        in_maps.append(m)
    res = run_bass_kernel_spmd(nc, in_maps, core_ids=list(range(NCORE)))
    out = np.concatenate([r["y"] for r in res.results], axis=0)
    return out.reshape(2, 4096, 4096)


# version 07 (final)
# speedup vs baseline: 1.0763x; 1.0763x over previous
import numpy as np
import concourse.bass as bass
import concourse.mybir as mybir
from concourse.bass_utils import run_bass_kernel_spmd

F32 = mybir.dt.float32
F16 = mybir.dt.float16
U8 = mybir.dt.uint8
AT = mybir.ActivationFunctionType
OP = mybir.AluOpType

H = 4096
I = 11008
O = 4096
NCORE = 8
TOK = 1024
CH = 2
CT = 512
NT = 4
KT = H // 128
IT = I // 128
WINS = [(w, min(512, I - w)) for w in range(0, I, 512)]
NW = len(WINS)
NWV = O // 512
SUB = 8
ITERS = 18
SEL_L = 4608
SEL_R = I - SEL_L
K_TOP = I // 2
K_EFF = float(K_TOP) - SEL_R / 2.0
EPS = 1e-5
RING = 3
SLOTB = 4096


def _subs(nk):
    out, k = [], 0
    while k < nk:
        c = min(SUB, nk - k)
        out.append((k, c))
        k += c
    return out


SUBS_KT = _subs(KT)
SUBS_IT = _subs(IT)

ITEMS = []
for _w, _wl in WINS:
    for _k0, _nk in SUBS_KT:
        ITEMS.append(("g", (_w, _wl, _k0, _nk)))
ITEMS.append(("z", ()))
for _k0, _nk in SUBS_KT:
    ITEMS.append(("a", (_k0, _nk)))
for _wv in range(NWV):
    for _k0, _nk in SUBS_KT:
        ITEMS.append(("b", (_wv, _k0, _nk)))
    ITEMS.append(("bb", (_wv,)))
    ITEMS.append(("bw", (_wv,)))
for _w, _wl in WINS:
    for _k0, _nk in SUBS_KT:
        ITEMS.append(("u", (_w, _wl, _k0, _nk)))
for _hw in range(NWV):
    for _k0, _nk in SUBS_IT:
        ITEMS.append(("d", (_hw, _k0, _nk)))
NITEM = len(ITEMS)

NPT = 128 + NT + NT * IT


def build_program():
    nc = bass.Bass("TRN2", target_bir_lowering=False, debug=False)

    x_h = nc.dram_tensor("x", [TOK, H], F32, kind="ExternalInput").ap()
    wgT_h = nc.dram_tensor("wgT", [H, I], F32, kind="ExternalInput").ap()
    wuT_h = nc.dram_tensor("wuT", [H, I], F32, kind="ExternalInput").ap()
    wdT_h = nc.dram_tensor("wdT", [I, O], F32, kind="ExternalInput").ap()
    wbT_h = nc.dram_tensor("wbT", [H, O], F32, kind="ExternalInput").ap()
    bb_h = nc.dram_tensor("bb", [1, O], F32, kind="ExternalInput").ap()
    wz_h = nc.dram_tensor("wzthrT", [H, 9], F32, kind="ExternalInput").ap()
    bthr_h = nc.dram_tensor("bthr", [128, 1], F32, kind="ExternalInput").ap()
    awT_h = nc.dram_tensor("awT", [H, 128], F32, kind="ExternalInput").ap()
    bwT_h = nc.dram_tensor("bwT", [128, O], F32, kind="ExternalInput").ap()
    eye_h = nc.dram_tensor("eye", [128, 128], F32, kind="ExternalInput").ap()
    y_h = nc.dram_tensor("y", [TOK, H], F32, kind="ExternalOutput").ap()
    ffns_h = nc.dram_tensor("ffn_scr", [NT, NWV, 128, 512], F16,
                            kind="Internal").ap()

    def item_src(kind, meta):
        if kind in ("g", "u"):
            w, wl, k0, nk = meta
            hb = wgT_h if kind == "g" else wuT_h
            return hb[k0 * 128:(k0 + nk) * 128, w:w + wl].rearrange(
                "(t p) n -> p t n", p=128), nk, wl, False
        if kind == "b":
            wv, k0, nk = meta
            return wbT_h[k0 * 128:(k0 + nk) * 128,
                         wv * 512:(wv + 1) * 512].rearrange(
                "(t p) n -> p t n", p=128), nk, 512, False
        if kind == "d":
            hw, k0, nk = meta
            return wdT_h[k0 * 128:(k0 + nk) * 128,
                         hw * 512:(hw + 1) * 512].rearrange(
                "(t p) n -> p t n", p=128), nk, 512, False
        if kind == "z":
            return wz_h.rearrange("(t p) n -> p t n", p=128), KT, 9, False
        if kind == "a":
            k0, nk = meta
            return awT_h[k0 * 128:(k0 + nk) * 128, :].rearrange(
                "(t p) n -> p t n", p=128), nk, 128, False
        if kind == "bw":
            wv, = meta
            return bwT_h[:, wv * 512:(wv + 1) * 512], 1, 512, False
        if kind == "bb":
            wv, = meta
            return bb_h[:, wv * 512:(wv + 1) * 512], 1, 512, True
        raise AssertionError(kind)

    def n_pG(c, wi, m):
        return c * (NW * NT) + wi * NT + m + 1

    def n_pB(c, wv, m):
        return c * (NWV * 3) + wv * 3 + max(m, 1)

    def n_pU(c, wi, m):
        return c * (NW * NT) + wi * NT + m + 1

    def n_pD(c, hw, m):
        return c * (NWV * NT) + hw * NT + m + 1

    def n_vPC(c, wi, m):
        return c * (NW * NT) + wi * NT + m + 1

    def n_vMID(c, m, it):
        return c * NT * (ITERS + 1) + m * (ITERS + 1) + it + 1

    def n_vMK(c, m):
        return c * NT + m + 1

    def n_vXN(c, m):
        return c * NT + m + 1

    def n_vVAR(c, m):
        return c * NT + m + 1

    def n_aRS(c, m):
        return c * NT + m + 1

    def n_aSQ(c, m):
        return c * (2 * NT) + 2 * m + 2

    def n_aGE(c, wi, m):
        return c * (NW * NT) + wi * NT + m + 1

    def n_aCR(c, m, it):
        return c * NT * ITERS + m * ITERS + it + 1

    def n_aFE(c, wv, m):
        return c * (NWV * NT) + wv * NT + m + 1

    def n_aPT(i):
        return i + 1

    def pt_idx(c, phase, a, b=0):
        base = c * NPT
        if phase == "ln":
            return base + a * KT + b
        if phase == "hw":
            return base + 128 + a
        return base + 128 + NT + a * IT + b

    def n_vEP(c, hw, m):
        return c * (NWV * NT) + hw * NT + m + 1

    from contextlib import ExitStack
    with ExitStack() as ctx:
        en = ctx.enter_context

        def sbuf(name, shape, dt):
            return en(nc.sbuf_tensor(name, shape, dt))

        S = sbuf("S", [128, NT * I], F16)
        xT = sbuf("xT", [128, KT * 512], F16)
        sc27 = sbuf("sc27", [128, I], F16)
        ovr = sbuf("ovr", [128, I], F16)
        wst = sbuf("wst", [128, RING * SLOTB], F16)
        xn16 = sbuf("xn16", [128, H], F16)
        ffst = sbuf("ffst", [128, 4 * 512], F16)
        xs = sbuf("xs", [128, 2 * 512], F32)
        eye16 = sbuf("eye16", [128, 128], F16)
        ones16 = sbuf("ones16", [1, 128], F16)
        h16 = sbuf("h16", [128, NT * 128], F16)
        hwT16 = sbuf("hwT16", [128, NT * 128], F16)
        g9f = sbuf("g9f", [128, NT * 9], F32)
        e8 = sbuf("e8", [128, NT * 8], F32)
        w8 = sbuf("w8", [128, NT * 8], F32)
        wn = sbuf("wn", [128, NT * 8], F32)
        mx4 = sbuf("mx4", [128, NT], F32)
        nmx4 = sbuf("nmx4", [128, NT], F32)
        se4 = sbuf("se4", [128, NT], F32)
        rse4 = sbuf("rse4", [128, NT], F32)
        th4 = sbuf("th4", [128, NT], F32)
        thv4 = sbuf("thv4", [128, NT], F32)
        ws4 = sbuf("ws4", [128, NT], F32)
        wz4 = sbuf("wz4", [128, NT], F32)
        wr2 = sbuf("wr2", [128, NT], F32)
        bthr2 = sbuf("bthr2", [128, 1], F32)
        bthr2h = sbuf("bthr2h", [128, 1], F32)
        st = sbuf("st", [128, 3 * NT], F32)
        cl = sbuf("cl", [128, 2], F32)
        cu = sbuf("cu", [128, 1], F32)
        pred = sbuf("pred", [128, 2], U8)
        lns = sbuf("lns", [128, 52 * NT], F32)

        sab = ovr[:, 0:I]
        ovr32 = ovr.bitcast(F32)
        xbuf = ovr32[:, 0:H]
        out32 = [ovr32[:, H + p * 512:H + (p + 1) * 512] for p in range(2)]
        sc32 = sc27.bitcast(F32)
        junkL = sc27[:, 0:SEL_L]
        junkR = sc27[:, SEL_L:I]

        quad = [en(nc.psum_tensor(f"q{i}", [128, 512], F32)) for i in range(6)]
        pt16 = [en(nc.psum_tensor(f"pt{i}", [128, 128], F16)) for i in range(2)]
        NPTB = 2

        def sem(name):
            return en(nc.semaphore(name))

        dX = sem("dX"); dCON = sem("dCON")
        dW = [sem(f"dW{i}") for i in range(RING)]
        dXS = [sem("dXS0"), sem("dXS1")]
        dFW = [sem("dFW0"), sem("dFW1")]
        dFR = [sem("dFR0"), sem("dFR1")]
        dY = [sem("dY0"), sem("dY1")]
        pLT = sem("pLT"); pG = sem("pG"); pGH = sem("pGH"); pHT = sem("pHT")
        pB = sem("pB"); pU = sem("pU"); pST = sem("pST"); pD = sem("pD")
        pWF = sem("pWF")
        aPT = sem("aPT"); aGE = sem("aGE"); aCR = sem("aCR")
        aEXP = sem("aEXP"); aHWT = sem("aHWT"); aFE = sem("aFE")
        aSE = sem("aSE"); aSQ = sem("aSQ"); aRS = sem("aRS")
        vXN = sem("vXN"); vVAR = sem("vVAR"); vG9 = sem("vG9")
        vNMX = sem("vNMX"); vGA = sem("vGA")
        vMID = sem("vMID"); vMK = sem("vMK"); vPC = sem("vPC"); vEP = sem("vEP")
        cD = sem("cD"); cA = sem("cA")

        block = en(nc.Block())

        @block.sync
        def _(sync):
            for c in range(CH):
                for m in range(NT):
                    if c * NT + m > 0:
                        sync.wait_ge(vXN, c * NT + m)
                    if m == 0 and c > 0:
                        sync.wait_ge(vEP, c * NWV * NT)
                    r0 = c * CT + m * 128
                    sync.dma_start(out=xbuf, in_=x_h[r0:r0 + 128, :]
                                   ).then_inc(dX, 16)
                for wv in range(NWV):
                    for m in range(NT):
                        j = wv * NT + m
                        p = j % 2
                        sync.wait_ge(aFE, c * 32 + j + 1)
                        sync.dma_start(out=ffns_h[m, wv, :, :],
                                       in_=ffst[:, p * 512:(p + 1) * 512]
                                       ).then_inc(dFW[p], 16)
                sync.wait_ge(vMK, c * NT + NT)

                def ep_read(j):
                    hw, m = divmod(j, NT)
                    p = j % 2
                    if c * 32 + j - 2 >= 0 and j >= 2:
                        sync.wait_ge(vEP, c * 32 + j - 1)
                    sync.wait_ge(dFW[p], 16 * (c * 16 + j // 2 + 1))
                    sync.dma_start(
                        out=ffst[:, (2 + p) * 512:(3 + p) * 512],
                        in_=ffns_h[m, hw, :, :]).then_inc(dFR[p], 16)
                    r0 = c * CT + m * 128
                    sync.dma_start(
                        out=xs[:, p * 512:(p + 1) * 512],
                        in_=x_h[r0:r0 + 128, hw * 512:(hw + 1) * 512]
                    ).then_inc(dXS[p], 16)

                def ep_store(j):
                    hw, m = divmod(j, NT)
                    p = j % 2
                    sync.wait_ge(vEP, c * 32 + j + 1)
                    r0 = c * CT + m * 128
                    sync.dma_start(
                        out=y_h[r0:r0 + 128, hw * 512:(hw + 1) * 512],
                        in_=out32[p]).then_inc(dY[p], 16)

                ep_read(0)
                ep_read(1)
                for j in range(NWV * NT):
                    ep_store(j)
                    if j + 2 < NWV * NT:
                        ep_read(j + 2)
            sync.wait_ge(dY[0], 16 * (CH * 16))
            sync.wait_ge(dY[1], 16 * (CH * 16))

        @block.gpsimd
        def _(gpsimd):
            gpsimd.dma_start(out=eye16[:, :], in_=eye_h).then_inc(dCON, 16)
            gpsimd.dma_start(out=bthr2[:, :], in_=bthr_h).then_inc(dCON, 16)
            for c in range(CH):
                for j, (kind, meta) in enumerate(ITEMS):
                    gj = c * NITEM + j
                    slot = gj % RING
                    if gj - (RING - 1) > 0:
                        gpsimd.wait_ge(pWF, gj - (RING - 1))
                    src, t, n, is_1p = item_src(kind, meta)
                    if kind in ("bw",):
                        dst = wst[:, slot * SLOTB:slot * SLOTB + 512]
                    elif kind == "bb":
                        dst = wst[0:1, slot * SLOTB:slot * SLOTB + 512]
                    else:
                        dst = wst[:, slot * SLOTB:slot * SLOTB + t * n
                                  ].rearrange("p (t n) -> p t n", t=t)
                    gpsimd.dma_start(out=dst, in_=src).then_inc(dW[slot], 16)

        @block.tensor
        def _(tensor):
            tensor.wait_ge(dCON, 32)
            fpend = [0]
            gseen = [0]

            def wait_item(gj):
                slot = gj % RING
                tensor.wait_ge(dW[slot], 16 * (gj // RING + 1))
                if gseen[0] > 0:
                    fpend[0] += 1
                gseen[0] += 1
                return slot

            def mark(mm):
                if fpend[0] > 0 and mm is not None:
                    mm.then_inc(pWF, fpend[0])
                    fpend[0] = 0
                return mm

            def pt_wait(i):
                if i >= NPTB:
                    tensor.wait_ge(aPT, i - (NPTB - 1))

            for c in range(CH):
                for m in range(NT):
                    tensor.wait_ge(vXN, n_vXN(c, m))
                    for k in range(KT):
                        i = pt_idx(c, "ln", m, k)
                        pt_wait(i)
                        tensor.transpose(pt16[i % NPTB][:, :],
                                         xn16[:, k * 128:(k + 1) * 128],
                                         eye16[:, :]).then_inc(pLT, 1)
                gj = c * NITEM
                for wi, (w, wl) in enumerate(WINS):
                    for k0, nk in SUBS_KT:
                        slot = wait_item(gj)
                        if wi == 0 and k0 == 0:
                            tensor.wait_ge(aPT, c * NPT + 128)
                        for kk in range(nk):
                            k = k0 + kk
                            for m in range(NT):
                                if k == 0 and wi > 0:
                                    tensor.wait_ge(aGE, n_aGE(c, wi - 1, m))
                                elif k == 0 and c > 0:
                                    tensor.wait_ge(vEP,
                                                   n_vEP(c - 1, NWV - 1, m))
                                mm = mark(tensor.matmul(
                                    quad[m][:, 0:wl],
                                    xT[:, k * 512 + m * 128:
                                       k * 512 + (m + 1) * 128],
                                    wst[:, slot * SLOTB + kk * wl:
                                        slot * SLOTB + (kk + 1) * wl],
                                    start=(k == 0), stop=(k == KT - 1)))
                                mm.annotate("mm_gate")
                                if k == KT - 1:
                                    mm.then_inc(pG, 1)
                        gj += 1
                slot = wait_item(gj)
                for k in range(KT):
                    for m in range(NT):
                        if k == 0:
                            tensor.wait_ge(aGE, n_aGE(c, NW - 1, m))
                        mm = mark(tensor.matmul(
                            quad[m][:, 0:9],
                            xT[:, k * 512 + m * 128:k * 512 + (m + 1) * 128],
                            wst[:, slot * SLOTB + k * 9:
                                slot * SLOTB + (k + 1) * 9],
                            start=(k == 0), stop=(k == KT - 1)))
                mm.then_inc(pGH, 1)
                gj += 1
                for k0, nk in SUBS_KT:
                    slot = wait_item(gj)
                    for kk in range(nk):
                        k = k0 + kk
                        for m in range(NT):
                            if k == 0:
                                tensor.wait_ge(vG9, c * 8 + m + 1)
                            mm = mark(tensor.matmul(
                                quad[m][:, 0:128],
                                xT[:, k * 512 + m * 128:
                                   k * 512 + (m + 1) * 128],
                                wst[:, slot * SLOTB + kk * 128:
                                    slot * SLOTB + (kk + 1) * 128],
                                start=(k == 0), stop=(k == KT - 1)))
                    gj += 1
                mm.then_inc(pGH, 1)
                tensor.wait_ge(vGA, c + 1)
                for m in range(NT):
                    i = pt_idx(c, "hw", m)
                    pt_wait(i)
                    tensor.transpose(pt16[i % NPTB][:, :],
                                     h16[:, m * 128:(m + 1) * 128],
                                     eye16[:, :]).then_inc(pHT, 1)
                for wv in range(NWV):
                    for k0, nk in SUBS_KT:
                        slot = wait_item(gj)
                        for kk in range(nk):
                            k = k0 + kk
                            for m in range(NT):
                                if k == 0:
                                    if wv > 0:
                                        tensor.wait_ge(aFE, n_aFE(c, wv - 1, m))
                                    else:
                                        tensor.wait_ge(vG9, c * 8 + 4 + m + 1)
                                mm = mark(tensor.matmul(
                                    quad[m][:, 0:512],
                                    xT[:, k * 512 + m * 128:
                                       k * 512 + (m + 1) * 128],
                                    wst[:, slot * SLOTB + kk * 512:
                                        slot * SLOTB + (kk + 1) * 512],
                                    start=(k == 0), stop=False))
                                mm.annotate("mm_base")
                        gj += 1
                    slot_bb = wait_item(gj)
                    gj += 1
                    if wv == 0:
                        tensor.wait_ge(aPT, c * NPT + 128 + NT)
                    for m in range(NT):
                        mark(tensor.matmul(quad[m][:, 0:512], ones16[:, :],
                                           wst[0:1, slot_bb * SLOTB:
                                               slot_bb * SLOTB + 512],
                                           start=False, stop=False))
                    slot_bw = wait_item(gj)
                    gj += 1
                    for m in range(NT):
                        mm = tensor.matmul(
                            quad[m][:, 0:512],
                            hwT16[:, m * 128:(m + 1) * 128],
                            wst[:, slot_bw * SLOTB:slot_bw * SLOTB + 512],
                            start=False, stop=True)
                        if m == 0:
                            mark(mm)
                        else:
                            mm.then_inc(pB, 1)
                for wi, (w, wl) in enumerate(WINS):
                    for k0, nk in SUBS_KT:
                        slot = wait_item(gj)
                        for kk in range(nk):
                            k = k0 + kk
                            for m in range(NT):
                                g = wi * NT + m
                                bk = quad[g % 6]
                                if k == 0:
                                    if g >= 6:
                                        tensor.wait_ge(vPC,
                                                       c * (NW * NT) + g - 5)
                                    else:
                                        if g < NT:
                                            tensor.wait_ge(aFE,
                                                           n_aFE(c, NWV - 1, g))
                                        elif c > 0:
                                            tensor.wait_ge(vEP, c * (NWV * NT))
                                mm = mark(tensor.matmul(
                                    bk[:, 0:wl],
                                    xT[:, k * 512 + m * 128:
                                       k * 512 + (m + 1) * 128],
                                    wst[:, slot * SLOTB + kk * wl:
                                        slot * SLOTB + (kk + 1) * wl],
                                    start=(k == 0), stop=(k == KT - 1)))
                                mm.annotate("mm_up")
                                if k == KT - 1:
                                    mm.then_inc(pU, 1)
                        gj += 1
                for m in range(NT):
                    tensor.wait_ge(vPC, n_vPC(c, NW - 1, m))
                    for kt in range(IT):
                        i = pt_idx(c, "su", m, kt)
                        pt_wait(i)
                        tensor.transpose(
                            pt16[i % NPTB][:, :],
                            S[:, m * I + kt * 128:m * I + (kt + 1) * 128],
                            eye16[:, :]).then_inc(pST, 1)
                for hw in range(NWV):
                    for k0, nk in SUBS_IT:
                        slot = wait_item(gj)
                        for kk in range(nk):
                            k = k0 + kk
                            for m in range(NT):
                                g = hw * NT + m
                                bk = quad[g % 6]
                                if k == 0:
                                    tensor.wait_ge(
                                        aPT,
                                        c * NPT + 128 + NT + (m + 1) * IT)
                                    if g >= 6:
                                        tensor.wait_ge(vEP,
                                                       c * (NWV * NT) + g - 5)
                                    else:
                                        tensor.wait_ge(
                                            vPC, c * (NW * NT) + NW * NT)
                                if m == 0:
                                    lhs = sc27[:, k * 128:(k + 1) * 128]
                                else:
                                    lhs = S[:, (m - 1) * I + k * 128:
                                            (m - 1) * I + (k + 1) * 128]
                                mm = mark(tensor.matmul(
                                    bk[:, 0:512], lhs,
                                    wst[:, slot * SLOTB + kk * 512:
                                        slot * SLOTB + (kk + 1) * 512],
                                    start=(k == 0), stop=(k == IT - 1)))
                                mm.annotate("mm_down")
                                if k == IT - 1:
                                    mm.then_inc(pD, 1)
                        gj += 1
                assert gj == (c + 1) * NITEM, (gj, NITEM)

        @block.scalar
        def _(scalar):
            napt = [0]

            for c in range(CH):
                scalar.preload_act_table(AT.Sqrt)
                for m in range(NT):
                    scalar.wait_ge(vVAR, n_vVAR(c, m))
                    scalar.activation(out=lns[:, 52 * m + 50:52 * m + 51],
                                      in_=lns[:, 52 * m + 49:52 * m + 50],
                                      func=AT.Sqrt).then_inc(aRS, 1)
                    for k in range(KT):
                        i = pt_idx(c, "ln", m, k)
                        scalar.wait_ge(pLT, c * 128 + m * KT + k + 1)
                        scalar.copy(
                            out=xT[:, k * 512 + m * 128:
                                   k * 512 + (m + 1) * 128],
                            in_=pt16[i % NPTB][:, :]).then_inc(aPT, 1)
                        napt[0] += 1
                scalar.preload_act_table(AT.Silu)
                for wi, (w, wl) in enumerate(WINS):
                    for m in range(NT):
                        scalar.wait_ge(pG, n_pG(c, wi, m))
                        scalar.activation(
                            out=S[:, m * I + w:m * I + w + wl],
                            in_=quad[m][:, 0:wl],
                            func=AT.Silu).then_inc(aGE, 1)
                scalar.preload_act_table(AT.Exp)
                scalar.wait_ge(vNMX, c + 1)
                for m in range(NT):
                    scalar.activation(
                        out=e8[:, m * 8:(m + 1) * 8],
                        in_=g9f[:, m * 9:m * 9 + 8],
                        func=AT.Exp, bias=nmx4[:, m:m + 1],
                        accum_out=se4[:, m:m + 1]).then_inc(aEXP, 1)
                scalar.activation(out=th4[:, :], in_=g9f[:, 8::9], func=AT.Tanh,
                                  bias=bthr2h[:, :], scale=0.5
                                  ).then_inc(aEXP, 1)
                scalar.activation(out=wz4[:, :], in_=g9f[:, 0:NT],
                                  func=AT.Identity
                                  ).then_inc(aEXP, 1)
                pend_hwt = list(range(NT))
                pend_ffn = [(wv, m) for wv in range(NWV) for m in range(NT)]

                def drain_hwt(mm):
                    i = pt_idx(c, "hw", mm)
                    scalar.wait_ge(pHT, c * NT + mm + 1)
                    scalar.copy(out=hwT16[:, mm * 128:(mm + 1) * 128],
                                in_=pt16[i % NPTB][:, :]).then_inc(aPT, 1)
                    napt[0] += 1

                def drain_ffn():
                    wv, m = pend_ffn.pop(0)
                    j = wv * NT + m
                    p = j % 2
                    scalar.wait_ge(pB, n_pB(c, wv, m))
                    iu = c * 16 + j // 2
                    if iu > 0:
                        scalar.wait_ge(dFW[p], 16 * iu)
                    scalar.copy(out=ffst[:, p * 512:(p + 1) * 512],
                                in_=quad[m][:, 0:512]).then_inc(aFE, 1)

                for m in range(NT):
                    if c * NT + m > 0:
                        scalar.wait_ge(vMK, c * NT + m)
                    if m == 0 and c > 0:
                        scalar.wait_ge(dY[0], 16 * (c * 16))
                        scalar.wait_ge(dY[1], 16 * (c * 16))
                    scalar.activation(out=sab, in_=S[:, m * I:(m + 1) * I],
                                      func=AT.Abs).then_inc(cA, 1)
                    for it in range(ITERS):
                        scalar.wait_ge(vMID, n_vMID(c, m, it))
                        if it == 0:
                            scalar.wait_ge(cA, c * NT + m + 1)
                        scalar.activation(
                            out=junkR, in_=sab[:, SEL_L:I],
                            func=AT.Sign, bias=st[:, 3 * m + 2:3 * m + 3],
                            scale=-1.0,
                            accum_out=cl[:, 1:2]).annotate("selR"
                            ).then_inc(aCR, 1)
                        if m == 0 and it < NT:
                            drain_hwt(pend_hwt.pop(0))
                        elif pend_ffn and (m >= 1 or it >= 4):
                            drain_ffn()
                            if pend_ffn:
                                drain_ffn()
                while pend_ffn:
                    drain_ffn()
                for m in range(NT):
                    for kt in range(IT):
                        i = pt_idx(c, "su", m, kt)
                        scalar.wait_ge(pST, c * NT * IT + m * IT + kt + 1)
                        if m == 0:
                            dst = sc27[:, kt * 128:(kt + 1) * 128]
                        else:
                            dst = S[:, (m - 1) * I + kt * 128:
                                    (m - 1) * I + (kt + 1) * 128]
                        scalar.copy(out=dst, in_=pt16[i % NPTB][:, :]
                                    ).then_inc(aPT, 1)
                        napt[0] += 1

        @block.vector
        def _(vector):
            nD = [0]
            last = [None]

            def di(ins):
                nD[0] += 1
                ins.then_inc(cD, 1)
                last[0] = ins
                return nD[0]

            def dc(ins, sem_, k=1):
                ins.then_inc(sem_, k)
                last[0] = ins
                return ins

            def dw(n):
                vector.wait_ge(cD, n)

            vector.memset(ones16[:, :], 1.0)
            vector.wait_ge(dCON, 32)
            n = di(vector.tensor_scalar(out=bthr2h[:, :], in0=bthr2[:, :], scalar1=0.5,
                                        scalar2=None, op0=OP.mult))
            for c in range(CH):
                for m in range(NT):
                    vector.wait_ge(dX, 16 * (c * NT + m + 1))
                    b0 = 52 * m
                    bn48 = lns[:, b0:b0 + 48]
                    mv = lns[:, b0 + 48:b0 + 50]
                    mean = lns[:, b0 + 48:b0 + 49]
                    var = lns[:, b0 + 49:b0 + 50]
                    sd = lns[:, b0 + 50:b0 + 51]
                    tmp = lns[:, b0 + 51:b0 + 52]
                    for q in range(8):
                        n = di(vector.bn_stats(lns[:, b0 + 6 * q:b0 + 6 * q + 6],
                                               xbuf[:, 512 * q:512 * (q + 1)]))
                    dw(n)
                    n = di(vector.bn_aggr(mv, bn48))
                    dw(n)
                    dc(vector.tensor_scalar(out=var, in0=var,
                                             scalar1=float(EPS),
                                             scalar2=None, op0=OP.add), vVAR)
                    vector.wait_ge(aRS, n_aRS(c, m))
                    n = di(vector.reciprocal(tmp, sd))
                    dw(n)
                    n = di(vector.tensor_tensor(out=sd, in0=tmp, in1=tmp,
                                                op=OP.mult))
                    dw(n)
                    n = di(vector.tensor_tensor(out=sd, in0=sd, in1=var,
                                                op=OP.mult))
                    dw(n)
                    n = di(vector.tensor_scalar(out=sd, in0=sd, scalar1=-0.5,
                                                scalar2=1.5, op0=OP.mult,
                                                op1=OP.add))
                    dw(n)
                    n = di(vector.tensor_tensor(out=sd, in0=tmp, in1=sd,
                                                op=OP.mult))
                    dw(n)
                    vector.wait_ge(pLT, c * 128 + m * KT)
                    dc(vector.tensor_scalar(out=xn16[:, :], in0=xbuf,
                                             scalar1=mean, scalar2=sd,
                                             op0=OP.subtract, op1=OP.mult),
                       vXN)
                vector.wait_ge(pGH, c * 2 + 1)
                for m in range(NT):
                    dc(vector.tensor_copy(g9f[:, m * 9:(m + 1) * 9],
                                          quad[m][:, 0:9]), vG9)
                vector.wait_ge(pGH, c * 2 + 2)
                for m in range(NT):
                    dc(vector.tensor_copy(h16[:, m * 128:(m + 1) * 128],
                                          quad[m][:, 0:128]), vG9)
                vector.wait_ge(vG9, c * 8 + 8)
                for m in range(NT):
                    n = di(vector.reduce_max(mx4[:, m:m + 1],
                                             g9f[:, m * 9:m * 9 + 8],
                                             axis=mybir.AxisListType.X))
                dw(n)
                dc(vector.tensor_scalar(out=nmx4[:, :], in0=mx4[:, :],
                                        scalar1=-1.0, scalar2=None,
                                        op0=OP.mult), vNMX)
                vector.wait_ge(aEXP, c * 6 + 6)
                n = di(vector.reciprocal(rse4[:, :], se4[:, :]))
                n = di(vector.tensor_scalar(out=thv4[:, :], in0=th4[:, :], scalar1=0.0625,
                                            scalar2=0.0625, op0=OP.mult,
                                            op1=OP.add))
                dw(n)
                for m in range(NT):
                    n = di(vector.tensor_scalar(
                        out=w8[:, m * 8:(m + 1) * 8],
                        in0=e8[:, m * 8:(m + 1) * 8],
                        scalar1=rse4[:, m:m + 1], scalar2=None, op0=OP.mult))
                dw(n)
                for m in range(NT):
                    n = di(vector.tensor_scalar(
                        out=w8[:, m * 8:(m + 1) * 8],
                        in0=w8[:, m * 8:(m + 1) * 8],
                        scalar1=thv4[:, m:m + 1], scalar2=0.0,
                        op0=OP.subtract, op1=OP.max))
                dw(n)
                for m in range(NT):
                    n = di(vector.reduce_sum(ws4[:, m:m + 1],
                                             w8[:, m * 8:(m + 1) * 8],
                                             axis=mybir.AxisListType.X))
                dw(n)
                n = di(vector.tensor_scalar(out=wz4[:, :], in0=ws4[:, :], scalar1=0.0,
                                            scalar2=None, op0=OP.is_equal))
                dw(n)
                n = di(vector.tensor_tensor(out=ws4[:, :], in0=ws4[:, :], in1=wz4[:, :],
                                            op=OP.add))
                dw(n)
                n = di(vector.reciprocal(wr2[:, :], ws4[:, :]))
                dw(n)
                n = di(vector.tensor_scalar(out=wr2[:, :], in0=wr2[:, :], scalar1=2.0,
                                            scalar2=None, op0=OP.mult))
                dw(n)
                for m in range(NT):
                    n = di(vector.tensor_scalar(
                        out=wn[:, m * 8:(m + 1) * 8],
                        in0=w8[:, m * 8:(m + 1) * 8],
                        scalar1=wr2[:, m:m + 1], scalar2=None, op0=OP.mult))
                dw(n)
                for m in range(NT):
                    for e in range(8):
                        ins = vector.tensor_scalar(
                            out=h16[:, m * 128 + e * 16:m * 128 + (e + 1) * 16],
                            in0=h16[:, m * 128 + e * 16:m * 128 + (e + 1) * 16],
                            scalar1=wn[:, m * 8 + e:m * 8 + e + 1],
                            scalar2=None, op0=OP.mult)
                        if m == NT - 1 and e == 7:
                            dc(ins, vGA)
                        else:
                            di(ins)
                for m in range(NT):
                    lo = st[:, 3 * m:3 * m + 1]
                    hi = st[:, 3 * m + 1:3 * m + 2]
                    mid = st[:, 3 * m + 2:3 * m + 3]
                    vector.wait_ge(aGE, n_aGE(c, NW - 1, m))
                    n = di(vector.memset(lo, 0.0))
                    n = di(vector.reduce_max(hi, S[:, m * I:(m + 1) * I],
                                             axis=mybir.AxisListType.X,
                                             apply_absolute_value=True))
                    dw(n)
                    n = di(vector.tensor_add(mid, lo, hi))
                    dw(n)
                    dc(vector.tensor_scalar(out=mid, in0=mid, scalar1=0.5,
                                             scalar2=None, op0=OP.mult), vMID)
                    for it in range(ITERS):
                        vector.wait_ge(vMID, n_vMID(c, m, it))
                        if it == 0:
                            vector.wait_ge(aCR, c * NT * ITERS + m * ITERS + 1)
                        ins_cnt = vector.tensor_scalar(
                            out=junkL, in0=sab[:, 0:SEL_L],
                            scalar1=mid, scalar2=0.0,
                            op0=OP.is_le, op1=OP.add,
                            accum_out=cl[:, 0:1])
                        ins_cnt.annotate("selL")
                        n = di(ins_cnt)
                        vector.wait_ge(aCR, n_aCR(c, m, it))
                        dw(n)
                        n = di(vector.scalar_tensor_tensor(
                            out=cu[:, :], in0=cl[:, 1:2], scalar=0.5,
                            in1=cl[:, 0:1], op0=OP.mult, op1=OP.add))
                        dw(n)
                        n = di(vector.tensor_scalar(
                            out=pred[:, 0:1], in0=cu[:, :], scalar1=K_EFF,
                            scalar2=None, op0=OP.is_ge))
                        n = di(vector.tensor_scalar(
                            out=pred[:, 1:2], in0=cu[:, :], scalar1=K_EFF,
                            scalar2=None, op0=OP.is_lt))
                        dw(n)
                        n = di(vector.copy_predicated(hi, pred[:, 0:1], mid))
                        n = di(vector.copy_predicated(lo, pred[:, 1:2], mid))
                        dw(n)
                        n = di(vector.tensor_add(mid, lo, hi))
                        dw(n)
                        dc(vector.tensor_scalar(
                            out=mid, in0=mid, scalar1=0.5, scalar2=None,
                            op0=OP.mult), vMID)
                    n = di(vector.tensor_scalar(
                        out=junkL, in0=sab[:, 0:SEL_L], scalar1=hi,
                        scalar2=None, op0=OP.is_gt))
                    n = di(vector.tensor_scalar(
                        out=junkR, in0=sab[:, SEL_L:I], scalar1=hi,
                        scalar2=None, op0=OP.is_gt))
                    dw(n)
                    di(vector.tensor_tensor(
                        out=S[:, m * I:m * I + SEL_L], in0=junkL,
                        in1=S[:, m * I:m * I + SEL_L], op=OP.mult))
                    dc(vector.tensor_tensor(
                        out=S[:, m * I + SEL_L:(m + 1) * I], in0=junkR,
                        in1=S[:, m * I + SEL_L:(m + 1) * I], op=OP.mult), vMK)
                for wi, (w, wl) in enumerate(WINS):
                    for m in range(NT):
                        g = wi * NT + m
                        if wi == 0:
                            vector.wait_ge(vMK, n_vMK(c, m))
                        vector.wait_ge(pU, n_pU(c, wi, m))
                        dc(vector.tensor_tensor(
                            out=S[:, m * I + w:m * I + w + wl],
                            in0=S[:, m * I + w:m * I + w + wl],
                            in1=quad[g % 6][:, 0:wl], op=OP.mult), vPC)
                for hw in range(NWV):
                    for m in range(NT):
                        j = hw * NT + m
                        p = j % 2
                        vector.wait_ge(pD, n_pD(c, hw, m))
                        vector.wait_ge(dFR[p], 16 * (c * 16 + j // 2 + 1))
                        vector.wait_ge(dXS[p], 16 * (c * 16 + j // 2 + 1))
                        iu = c * 16 + j // 2
                        if iu > 0:
                            vector.wait_ge(dY[p], 16 * iu)
                        n = di(vector.tensor_tensor(
                            out=out32[p], in0=quad[(hw * NT + m) % 6][:, 0:512],
                            in1=ffst[:, (2 + p) * 512:(3 + p) * 512],
                            op=OP.add))
                        dw(n)
                        dc(vector.tensor_tensor(
                            out=out32[p], in0=out32[p],
                            in1=xs[:, p * 512:(p + 1) * 512], op=OP.add), vEP)

    return nc


def kernel(L0_attn_x, L0_attn_w_gate, L0_attn_w_up, L0_attn_w_down,
           L0_ffn_W_base, L0_ffn_b_base, L0_ffn_W_gate, L0_ffn_W_thr,
           L0_ffn_b_thr, L0_ffn_A_w, L0_ffn_B_w):
    x = np.ascontiguousarray(np.asarray(L0_attn_x, dtype=np.float32)
                             ).reshape(NCORE * TOK, H)
    wgT = np.ascontiguousarray(np.asarray(L0_attn_w_gate, np.float32).T)
    wuT = np.ascontiguousarray(np.asarray(L0_attn_w_up, np.float32).T)
    wdT = np.ascontiguousarray(np.asarray(L0_attn_w_down, np.float32).T)
    wbT = np.ascontiguousarray(np.asarray(L0_ffn_W_base, np.float32).T)
    bb = np.ascontiguousarray(np.asarray(L0_ffn_b_base, np.float32)
                              ).reshape(1, O)
    wz = np.asarray(L0_ffn_W_gate, np.float32).T
    wthr = np.asarray(L0_ffn_W_thr, np.float32).reshape(H, 1)
    wzthrT = np.ascontiguousarray(np.concatenate([wz, wthr], axis=1))
    bthr = np.broadcast_to(np.asarray(L0_ffn_b_thr, np.float32).reshape(1, 1),
                           (128, 1)).copy()
    awT = np.ascontiguousarray(
        np.asarray(L0_ffn_A_w, np.float32).reshape(128, H).T)
    bwT = np.ascontiguousarray(
        np.asarray(L0_ffn_B_w, np.float32).transpose(0, 2, 1).reshape(128, O))
    eye = np.eye(128, dtype=np.float32)

    nc = build_program()
    shared = dict(wgT=wgT, wuT=wuT, wdT=wdT, wbT=wbT, bb=bb, wzthrT=wzthrT,
                  bthr=bthr, awT=awT, bwT=bwT, eye=eye)
    in_maps = []
    for cid in range(NCORE):
        m = dict(shared)
        m["x"] = np.ascontiguousarray(x[cid * TOK:(cid + 1) * TOK])
        in_maps.append(m)
    res = run_bass_kernel_spmd(nc, in_maps, core_ids=list(range(NCORE)))
    out = np.concatenate([r["y"] for r in res.results], axis=0)
    return out.reshape(2, 4096, 4096)


# version 08
# speedup vs baseline: 1.1342x; 1.0538x over previous
import numpy as np
import concourse.bass as bass
import concourse.mybir as mybir
from concourse.bass_utils import run_bass_kernel_spmd

F32 = mybir.dt.float32
F16 = mybir.dt.float16
U8 = mybir.dt.uint8
AT = mybir.ActivationFunctionType
OP = mybir.AluOpType

H = 4096
I = 11008
O = 4096
NCORE = 8
TOK = 1024
CH = 2
CT = 512
NT = 4
KT = H // 128
IT = I // 128
WINS = [(w, min(512, I - w)) for w in range(0, I, 512)]
NW = len(WINS)
NWV = O // 512
SUB = 8
ITERS = 16
SEL_L = 4608
SEL_R = I - SEL_L
K_TOP = I // 2
K_EFF = float(K_TOP) - SEL_R / 2.0
EPS = 1e-5
RING = 3
SLOTB = 4096


def _subs(nk):
    out, k = [], 0
    while k < nk:
        c = min(SUB, nk - k)
        out.append((k, c))
        k += c
    return out


SUBS_KT = _subs(KT)
SUBS_IT = _subs(IT)

ITEMS = []
for _w, _wl in WINS:
    for _k0, _nk in SUBS_KT:
        ITEMS.append(("g", (_w, _wl, _k0, _nk)))
ITEMS.append(("z", ()))
for _k0, _nk in SUBS_KT:
    ITEMS.append(("a", (_k0, _nk)))
for _wv in range(NWV):
    for _k0, _nk in SUBS_KT:
        ITEMS.append(("b", (_wv, _k0, _nk)))
    ITEMS.append(("bb", (_wv,)))
    ITEMS.append(("bw", (_wv,)))
for _w, _wl in WINS:
    for _k0, _nk in SUBS_KT:
        ITEMS.append(("u", (_w, _wl, _k0, _nk)))
for _hw in range(NWV):
    for _k0, _nk in SUBS_IT:
        ITEMS.append(("d", (_hw, _k0, _nk)))
NITEM = len(ITEMS)

NPT = 128 + NT + NT * IT


def build_program():
    nc = bass.Bass("TRN2", target_bir_lowering=False, debug=False)

    x_h = nc.dram_tensor("x", [TOK, H], F32, kind="ExternalInput").ap()
    wgT_h = nc.dram_tensor("wgT", [H, I], F32, kind="ExternalInput").ap()
    wuT_h = nc.dram_tensor("wuT", [H, I], F32, kind="ExternalInput").ap()
    wdT_h = nc.dram_tensor("wdT", [I, O], F32, kind="ExternalInput").ap()
    wbT_h = nc.dram_tensor("wbT", [H, O], F32, kind="ExternalInput").ap()
    bb_h = nc.dram_tensor("bb", [1, O], F32, kind="ExternalInput").ap()
    wz_h = nc.dram_tensor("wzthrT", [H, 9], F32, kind="ExternalInput").ap()
    bthr_h = nc.dram_tensor("bthr", [128, 1], F32, kind="ExternalInput").ap()
    awT_h = nc.dram_tensor("awT", [H, 128], F32, kind="ExternalInput").ap()
    bwT_h = nc.dram_tensor("bwT", [128, O], F32, kind="ExternalInput").ap()
    eye_h = nc.dram_tensor("eye", [128, 128], F32, kind="ExternalInput").ap()
    y_h = nc.dram_tensor("y", [TOK, H], F32, kind="ExternalOutput").ap()
    ffns_h = nc.dram_tensor("ffn_scr", [NT, NWV, 128, 512], F16,
                            kind="Internal").ap()

    def item_src(kind, meta):
        if kind in ("g", "u"):
            w, wl, k0, nk = meta
            hb = wgT_h if kind == "g" else wuT_h
            return hb[k0 * 128:(k0 + nk) * 128, w:w + wl].rearrange(
                "(t p) n -> p t n", p=128), nk, wl, False
        if kind == "b":
            wv, k0, nk = meta
            return wbT_h[k0 * 128:(k0 + nk) * 128,
                         wv * 512:(wv + 1) * 512].rearrange(
                "(t p) n -> p t n", p=128), nk, 512, False
        if kind == "d":
            hw, k0, nk = meta
            return wdT_h[k0 * 128:(k0 + nk) * 128,
                         hw * 512:(hw + 1) * 512].rearrange(
                "(t p) n -> p t n", p=128), nk, 512, False
        if kind == "z":
            return wz_h.rearrange("(t p) n -> p t n", p=128), KT, 9, False
        if kind == "a":
            k0, nk = meta
            return awT_h[k0 * 128:(k0 + nk) * 128, :].rearrange(
                "(t p) n -> p t n", p=128), nk, 128, False
        if kind == "bw":
            wv, = meta
            return bwT_h[:, wv * 512:(wv + 1) * 512], 1, 512, False
        if kind == "bb":
            wv, = meta
            return bb_h[:, wv * 512:(wv + 1) * 512], 1, 512, True
        raise AssertionError(kind)

    def n_pG(c, wi, m):
        return c * (NW * NT) + wi * NT + m + 1

    def n_pB(c, wv, m):
        return c * (NWV * 3) + wv * 3 + max(m, 1)

    def n_pU(c, wi, m):
        return c * (NW * NT) + wi * NT + m + 1

    def n_pD(c, hw, m):
        return c * (NWV * NT) + hw * NT + m + 1

    def n_vPC(c, wi, m):
        return c * (NW * NT) + wi * NT + m + 1

    def n_vMID(c, m, it):
        return c * NT * (ITERS + 1) + m * (ITERS + 1) + it + 1

    def n_vMK(c, m):
        return c * NT + m + 1

    def n_vXN(c, m):
        return c * NT + m + 1

    def n_vVAR(c, m):
        return c * NT + m + 1

    def n_aRS(c, m):
        return c * NT + m + 1

    def n_aSQ(c, m):
        return c * (2 * NT) + 2 * m + 2

    def n_aGE(c, wi, m):
        return c * (NW * NT) + wi * NT + m + 1

    def n_aCR(c, m, it):
        return c * NT * ITERS + m * ITERS + it + 1

    def n_aFE(c, wv, m):
        return c * (NWV * NT) + wv * NT + m + 1

    def n_aPT(i):
        return i + 1

    def pt_idx(c, phase, a, b=0):
        base = c * NPT
        if phase == "ln":
            return base + a * KT + b
        if phase == "hw":
            return base + 128 + a
        return base + 128 + NT + a * IT + b

    def n_vEP(c, hw, m):
        return c * (NWV * NT) + hw * NT + m + 1

    from contextlib import ExitStack
    with ExitStack() as ctx:
        en = ctx.enter_context

        def sbuf(name, shape, dt):
            return en(nc.sbuf_tensor(name, shape, dt))

        S = sbuf("S", [128, NT * I], F16)
        xT = sbuf("xT", [128, KT * 512], F16)
        sc27 = sbuf("sc27", [128, I], F16)
        ovr = sbuf("ovr", [128, I], F16)
        wst = sbuf("wst", [128, RING * SLOTB], F16)
        xn16 = sbuf("xn16", [128, H], F16)
        ffst = sbuf("ffst", [128, 4 * 512], F16)
        xs = sbuf("xs", [128, 2 * 512], F32)
        eye16 = sbuf("eye16", [128, 128], F16)
        ones16 = sbuf("ones16", [1, 128], F16)
        h16 = sbuf("h16", [128, NT * 128], F16)
        hwT16 = sbuf("hwT16", [128, NT * 128], F16)
        g9f = sbuf("g9f", [128, NT * 9], F32)
        e8 = sbuf("e8", [128, NT * 8], F32)
        w8 = sbuf("w8", [128, NT * 8], F32)
        wn = sbuf("wn", [128, NT * 8], F32)
        mx4 = sbuf("mx4", [128, NT], F32)
        nmx4 = sbuf("nmx4", [128, NT], F32)
        se4 = sbuf("se4", [128, NT], F32)
        rse4 = sbuf("rse4", [128, NT], F32)
        th4 = sbuf("th4", [128, NT], F32)
        thv4 = sbuf("thv4", [128, NT], F32)
        ws4 = sbuf("ws4", [128, NT], F32)
        wz4 = sbuf("wz4", [128, NT], F32)
        wr2 = sbuf("wr2", [128, NT], F32)
        bthr2 = sbuf("bthr2", [128, 1], F32)
        bthr2h = sbuf("bthr2h", [128, 1], F32)
        st = sbuf("st", [128, 3 * NT], F32)
        cl = sbuf("cl", [128, 2], F32)
        cu = sbuf("cu", [128, 1], F32)
        pred = sbuf("pred", [128, 2], U8)
        lns = sbuf("lns", [128, 52 * NT], F32)

        sab = ovr[:, 0:I]
        ovr32 = ovr.bitcast(F32)
        xbuf = ovr32[:, 0:H]
        out32 = [ovr32[:, H + p * 512:H + (p + 1) * 512] for p in range(2)]
        sc32 = sc27.bitcast(F32)
        junkL = sc27[:, 0:SEL_L]
        junkR = sc27[:, SEL_L:I]

        quad = [en(nc.psum_tensor(f"q{i}", [128, 512], F32)) for i in range(6)]
        pt16 = [en(nc.psum_tensor(f"pt{i}", [128, 128], F16)) for i in range(2)]
        NPTB = 2

        def sem(name):
            return en(nc.semaphore(name))

        dX = sem("dX"); dCON = sem("dCON")
        dW = [sem(f"dW{i}") for i in range(RING)]
        dXS = [sem("dXS0"), sem("dXS1")]
        dFW = [sem("dFW0"), sem("dFW1")]
        dFR = [sem("dFR0"), sem("dFR1")]
        dY = [sem("dY0"), sem("dY1")]
        pLT = sem("pLT"); pG = sem("pG"); pGH = sem("pGH"); pHT = sem("pHT")
        pB = sem("pB"); pU = sem("pU"); pST = sem("pST"); pD = sem("pD")
        pWF = sem("pWF")
        aPT = sem("aPT"); aGE = sem("aGE"); aCR = sem("aCR")
        aEXP = sem("aEXP"); aHWT = sem("aHWT"); aFE = sem("aFE")
        aSE = sem("aSE"); aSQ = sem("aSQ"); aRS = sem("aRS")
        vXN = sem("vXN"); vVAR = sem("vVAR"); vG9 = sem("vG9")
        vNMX = sem("vNMX"); vGA = sem("vGA")
        vMID = sem("vMID"); vMK = sem("vMK"); vPC = sem("vPC"); vEP = sem("vEP")
        cD = sem("cD"); cA = sem("cA")

        block = en(nc.Block())

        @block.sync
        def _(sync):
            for c in range(CH):
                for m in range(NT):
                    if c * NT + m > 0:
                        sync.wait_ge(vXN, c * NT + m)
                    if m == 0 and c > 0:
                        sync.wait_ge(vEP, c * NWV * NT)
                    r0 = c * CT + m * 128
                    sync.dma_start(out=xbuf, in_=x_h[r0:r0 + 128, :]
                                   ).then_inc(dX, 16)
                for wv in range(NWV):
                    for m in range(NT):
                        j = wv * NT + m
                        p = j % 2
                        sync.wait_ge(aFE, c * 32 + j + 1)
                        sync.dma_start(out=ffns_h[m, wv, :, :],
                                       in_=ffst[:, p * 512:(p + 1) * 512]
                                       ).then_inc(dFW[p], 16)
                sync.wait_ge(vMK, c * NT + NT)

                def ep_read(j):
                    hw, m = divmod(j, NT)
                    p = j % 2
                    if c * 32 + j - 2 >= 0 and j >= 2:
                        sync.wait_ge(vEP, c * 32 + j - 1)
                    sync.wait_ge(dFW[p], 16 * (c * 16 + j // 2 + 1))
                    sync.dma_start(
                        out=ffst[:, (2 + p) * 512:(3 + p) * 512],
                        in_=ffns_h[m, hw, :, :]).then_inc(dFR[p], 16)
                    r0 = c * CT + m * 128
                    sync.dma_start(
                        out=xs[:, p * 512:(p + 1) * 512],
                        in_=x_h[r0:r0 + 128, hw * 512:(hw + 1) * 512]
                    ).then_inc(dXS[p], 16)

                def ep_store(j):
                    hw, m = divmod(j, NT)
                    p = j % 2
                    sync.wait_ge(vEP, c * 32 + j + 1)
                    r0 = c * CT + m * 128
                    sync.dma_start(
                        out=y_h[r0:r0 + 128, hw * 512:(hw + 1) * 512],
                        in_=out32[p]).then_inc(dY[p], 16)

                ep_read(0)
                ep_read(1)
                for j in range(NWV * NT):
                    ep_store(j)
                    if j + 2 < NWV * NT:
                        ep_read(j + 2)
            sync.wait_ge(dY[0], 16 * (CH * 16))
            sync.wait_ge(dY[1], 16 * (CH * 16))

        @block.gpsimd
        def _(gpsimd):
            gpsimd.dma_start(out=eye16[:, :], in_=eye_h).then_inc(dCON, 16)
            gpsimd.dma_start(out=bthr2[:, :], in_=bthr_h).then_inc(dCON, 16)
            for c in range(CH):
                for j, (kind, meta) in enumerate(ITEMS):
                    gj = c * NITEM + j
                    slot = gj % RING
                    if gj - (RING - 1) > 0:
                        gpsimd.wait_ge(pWF, gj - (RING - 1))
                    src, t, n, is_1p = item_src(kind, meta)
                    if kind in ("bw",):
                        dst = wst[:, slot * SLOTB:slot * SLOTB + 512]
                    elif kind == "bb":
                        dst = wst[0:1, slot * SLOTB:slot * SLOTB + 512]
                    else:
                        dst = wst[:, slot * SLOTB:slot * SLOTB + t * n
                                  ].rearrange("p (t n) -> p t n", t=t)
                    gpsimd.dma_start(out=dst, in_=src).then_inc(dW[slot], 16)

        @block.tensor
        def _(tensor):
            tensor.wait_ge(dCON, 32)
            fpend = [0]
            gseen = [0]

            def wait_item(gj):
                slot = gj % RING
                tensor.wait_ge(dW[slot], 16 * (gj // RING + 1))
                if gseen[0] > 0:
                    fpend[0] += 1
                gseen[0] += 1
                return slot

            def mark(mm):
                if fpend[0] > 0 and mm is not None:
                    mm.then_inc(pWF, fpend[0])
                    fpend[0] = 0
                return mm

            def pt_wait(i):
                if i >= NPTB:
                    tensor.wait_ge(aPT, i - (NPTB - 1))

            for c in range(CH):
                for m in range(NT):
                    tensor.wait_ge(vXN, n_vXN(c, m))
                    for k in range(KT):
                        i = pt_idx(c, "ln", m, k)
                        pt_wait(i)
                        tensor.transpose(pt16[i % NPTB][:, :],
                                         xn16[:, k * 128:(k + 1) * 128],
                                         eye16[:, :]).then_inc(pLT, 1)
                gj = c * NITEM
                for wi, (w, wl) in enumerate(WINS):
                    for k0, nk in SUBS_KT:
                        slot = wait_item(gj)
                        if wi == 0 and k0 == 0:
                            tensor.wait_ge(aPT, c * NPT + 128)
                        for kk in range(nk):
                            k = k0 + kk
                            for m in range(NT):
                                if k == 0 and wi > 0:
                                    tensor.wait_ge(aGE, n_aGE(c, wi - 1, m))
                                elif k == 0 and c > 0:
                                    tensor.wait_ge(vEP,
                                                   n_vEP(c - 1, NWV - 1, m))
                                mm = mark(tensor.matmul(
                                    quad[m][:, 0:wl],
                                    xT[:, k * 512 + m * 128:
                                       k * 512 + (m + 1) * 128],
                                    wst[:, slot * SLOTB + kk * wl:
                                        slot * SLOTB + (kk + 1) * wl],
                                    start=(k == 0), stop=(k == KT - 1)))
                                mm.annotate("mm_gate")
                                if k == KT - 1:
                                    mm.then_inc(pG, 1)
                        gj += 1
                slot = wait_item(gj)
                for k in range(KT):
                    for m in range(NT):
                        if k == 0:
                            tensor.wait_ge(aGE, n_aGE(c, NW - 1, m))
                        mm = mark(tensor.matmul(
                            quad[m][:, 0:9],
                            xT[:, k * 512 + m * 128:k * 512 + (m + 1) * 128],
                            wst[:, slot * SLOTB + k * 9:
                                slot * SLOTB + (k + 1) * 9],
                            start=(k == 0), stop=(k == KT - 1)))
                mm.then_inc(pGH, 1)
                gj += 1
                for k0, nk in SUBS_KT:
                    slot = wait_item(gj)
                    for kk in range(nk):
                        k = k0 + kk
                        for m in range(NT):
                            if k == 0:
                                tensor.wait_ge(vG9, c * 8 + m + 1)
                            mm = mark(tensor.matmul(
                                quad[m][:, 0:128],
                                xT[:, k * 512 + m * 128:
                                   k * 512 + (m + 1) * 128],
                                wst[:, slot * SLOTB + kk * 128:
                                    slot * SLOTB + (kk + 1) * 128],
                                start=(k == 0), stop=(k == KT - 1)))
                    gj += 1
                mm.then_inc(pGH, 1)
                tensor.wait_ge(vGA, c + 1)
                for m in range(NT):
                    i = pt_idx(c, "hw", m)
                    pt_wait(i)
                    tensor.transpose(pt16[i % NPTB][:, :],
                                     h16[:, m * 128:(m + 1) * 128],
                                     eye16[:, :]).then_inc(pHT, 1)
                for wv in range(NWV):
                    for k0, nk in SUBS_KT:
                        slot = wait_item(gj)
                        for kk in range(nk):
                            k = k0 + kk
                            for m in range(NT):
                                if k == 0:
                                    if wv > 0:
                                        tensor.wait_ge(aFE, n_aFE(c, wv - 1, m))
                                    else:
                                        tensor.wait_ge(vG9, c * 8 + 4 + m + 1)
                                mm = mark(tensor.matmul(
                                    quad[m][:, 0:512],
                                    xT[:, k * 512 + m * 128:
                                       k * 512 + (m + 1) * 128],
                                    wst[:, slot * SLOTB + kk * 512:
                                        slot * SLOTB + (kk + 1) * 512],
                                    start=(k == 0), stop=False))
                                mm.annotate("mm_base")
                        gj += 1
                    slot_bb = wait_item(gj)
                    gj += 1
                    if wv == 0:
                        tensor.wait_ge(aPT, c * NPT + 128 + NT)
                    for m in range(NT):
                        mark(tensor.matmul(quad[m][:, 0:512], ones16[:, :],
                                           wst[0:1, slot_bb * SLOTB:
                                               slot_bb * SLOTB + 512],
                                           start=False, stop=False))
                    slot_bw = wait_item(gj)
                    gj += 1
                    for m in range(NT):
                        mm = tensor.matmul(
                            quad[m][:, 0:512],
                            hwT16[:, m * 128:(m + 1) * 128],
                            wst[:, slot_bw * SLOTB:slot_bw * SLOTB + 512],
                            start=False, stop=True)
                        if m == 0:
                            mark(mm)
                        else:
                            mm.then_inc(pB, 1)
                for wi, (w, wl) in enumerate(WINS):
                    for k0, nk in SUBS_KT:
                        slot = wait_item(gj)
                        for kk in range(nk):
                            k = k0 + kk
                            for m in range(NT):
                                g = wi * NT + m
                                bk = quad[g % 6]
                                if k == 0:
                                    if g >= 6:
                                        tensor.wait_ge(vPC,
                                                       c * (NW * NT) + g - 5)
                                    else:
                                        if g < NT:
                                            tensor.wait_ge(aFE,
                                                           n_aFE(c, NWV - 1, g))
                                        elif c > 0:
                                            tensor.wait_ge(vEP, c * (NWV * NT))
                                mm = mark(tensor.matmul(
                                    bk[:, 0:wl],
                                    xT[:, k * 512 + m * 128:
                                       k * 512 + (m + 1) * 128],
                                    wst[:, slot * SLOTB + kk * wl:
                                        slot * SLOTB + (kk + 1) * wl],
                                    start=(k == 0), stop=(k == KT - 1)))
                                mm.annotate("mm_up")
                                if k == KT - 1:
                                    mm.then_inc(pU, 1)
                        gj += 1
                for m in range(NT):
                    tensor.wait_ge(vPC, n_vPC(c, NW - 1, m))
                    for kt in range(IT):
                        i = pt_idx(c, "su", m, kt)
                        pt_wait(i)
                        tensor.transpose(
                            pt16[i % NPTB][:, :],
                            S[:, m * I + kt * 128:m * I + (kt + 1) * 128],
                            eye16[:, :]).then_inc(pST, 1)
                for hw in range(NWV):
                    for k0, nk in SUBS_IT:
                        slot = wait_item(gj)
                        for kk in range(nk):
                            k = k0 + kk
                            for m in range(NT):
                                g = hw * NT + m
                                bk = quad[g % 6]
                                if k == 0:
                                    tensor.wait_ge(
                                        aPT,
                                        c * NPT + 128 + NT + (m + 1) * IT)
                                    if g >= 6:
                                        tensor.wait_ge(vEP,
                                                       c * (NWV * NT) + g - 5)
                                    else:
                                        tensor.wait_ge(
                                            vPC, c * (NW * NT) + NW * NT)
                                if m == 0:
                                    lhs = sc27[:, k * 128:(k + 1) * 128]
                                else:
                                    lhs = S[:, (m - 1) * I + k * 128:
                                            (m - 1) * I + (k + 1) * 128]
                                mm = mark(tensor.matmul(
                                    bk[:, 0:512], lhs,
                                    wst[:, slot * SLOTB + kk * 512:
                                        slot * SLOTB + (kk + 1) * 512],
                                    start=(k == 0), stop=(k == IT - 1)))
                                mm.annotate("mm_down")
                                if k == IT - 1:
                                    mm.then_inc(pD, 1)
                        gj += 1
                assert gj == (c + 1) * NITEM, (gj, NITEM)

        @block.scalar
        def _(scalar):
            napt = [0]

            for c in range(CH):
                scalar.preload_act_table(AT.Sqrt)
                for m in range(NT):
                    scalar.wait_ge(vVAR, n_vVAR(c, m))
                    scalar.activation(out=lns[:, 52 * m + 50:52 * m + 51],
                                      in_=lns[:, 52 * m + 49:52 * m + 50],
                                      func=AT.Sqrt).then_inc(aRS, 1)
                    for k in range(KT):
                        i = pt_idx(c, "ln", m, k)
                        scalar.wait_ge(pLT, c * 128 + m * KT + k + 1)
                        scalar.copy(
                            out=xT[:, k * 512 + m * 128:
                                   k * 512 + (m + 1) * 128],
                            in_=pt16[i % NPTB][:, :]).then_inc(aPT, 1)
                        napt[0] += 1
                scalar.preload_act_table(AT.Silu)
                for wi, (w, wl) in enumerate(WINS):
                    for m in range(NT):
                        scalar.wait_ge(pG, n_pG(c, wi, m))
                        scalar.activation(
                            out=S[:, m * I + w:m * I + w + wl],
                            in_=quad[m][:, 0:wl],
                            func=AT.Silu).then_inc(aGE, 1)
                scalar.preload_act_table(AT.Exp)
                scalar.wait_ge(vNMX, c + 1)
                for m in range(NT):
                    scalar.activation(
                        out=e8[:, m * 8:(m + 1) * 8],
                        in_=g9f[:, m * 9:m * 9 + 8],
                        func=AT.Exp, bias=nmx4[:, m:m + 1],
                        accum_out=se4[:, m:m + 1]).then_inc(aEXP, 1)
                scalar.activation(out=th4[:, :], in_=g9f[:, 8::9], func=AT.Tanh,
                                  bias=bthr2h[:, :], scale=0.5
                                  ).then_inc(aEXP, 1)
                scalar.activation(out=wz4[:, :], in_=g9f[:, 0:NT],
                                  func=AT.Identity
                                  ).then_inc(aEXP, 1)
                pend_hwt = list(range(NT))
                pend_ffn = [(wv, m) for wv in range(NWV) for m in range(NT)]

                def drain_hwt(mm):
                    i = pt_idx(c, "hw", mm)
                    scalar.wait_ge(pHT, c * NT + mm + 1)
                    scalar.copy(out=hwT16[:, mm * 128:(mm + 1) * 128],
                                in_=pt16[i % NPTB][:, :]).then_inc(aPT, 1)
                    napt[0] += 1

                def drain_ffn():
                    wv, m = pend_ffn.pop(0)
                    j = wv * NT + m
                    p = j % 2
                    scalar.wait_ge(pB, n_pB(c, wv, m))
                    iu = c * 16 + j // 2
                    if iu > 0:
                        scalar.wait_ge(dFW[p], 16 * iu)
                    scalar.copy(out=ffst[:, p * 512:(p + 1) * 512],
                                in_=quad[m][:, 0:512]).then_inc(aFE, 1)

                for m in range(NT):
                    if c * NT + m > 0:
                        scalar.wait_ge(vMK, c * NT + m)
                    if m == 0 and c > 0:
                        scalar.wait_ge(dY[0], 16 * (c * 16))
                        scalar.wait_ge(dY[1], 16 * (c * 16))
                    scalar.activation(out=sab, in_=S[:, m * I:(m + 1) * I],
                                      func=AT.Abs).then_inc(cA, 1)
                    for it in range(ITERS):
                        scalar.wait_ge(vMID, n_vMID(c, m, it))
                        if it == 0:
                            scalar.wait_ge(cA, c * NT + m + 1)
                        scalar.activation(
                            out=junkR, in_=sab[:, SEL_L:I],
                            func=AT.Sign, bias=st[:, 3 * m + 2:3 * m + 3],
                            scale=-1.0,
                            accum_out=cl[:, 1:2]).annotate("selR"
                            ).then_inc(aCR, 1)
                        if m == 0 and it < NT:
                            drain_hwt(pend_hwt.pop(0))
                        elif pend_ffn and (m >= 1 or it >= 4):
                            drain_ffn()
                            if pend_ffn:
                                drain_ffn()
                while pend_ffn:
                    drain_ffn()
                for m in range(NT):
                    for kt in range(IT):
                        i = pt_idx(c, "su", m, kt)
                        scalar.wait_ge(pST, c * NT * IT + m * IT + kt + 1)
                        if m == 0:
                            dst = sc27[:, kt * 128:(kt + 1) * 128]
                        else:
                            dst = S[:, (m - 1) * I + kt * 128:
                                    (m - 1) * I + (kt + 1) * 128]
                        scalar.copy(out=dst, in_=pt16[i % NPTB][:, :]
                                    ).then_inc(aPT, 1)
                        napt[0] += 1

        @block.vector
        def _(vector):
            nD = [0]
            last = [None]

            def di(ins):
                nD[0] += 1
                ins.then_inc(cD, 1)
                last[0] = ins
                return nD[0]

            def dc(ins, sem_, k=1):
                ins.then_inc(sem_, k)
                last[0] = ins
                return ins

            def dw(n):
                vector.wait_ge(cD, n)

            vector.memset(ones16[:, :], 1.0)
            vector.wait_ge(dCON, 32)
            n = di(vector.tensor_scalar(out=bthr2h[:, :], in0=bthr2[:, :], scalar1=0.5,
                                        scalar2=None, op0=OP.mult))
            for c in range(CH):
                for m in range(NT):
                    vector.wait_ge(dX, 16 * (c * NT + m + 1))
                    b0 = 52 * m
                    bn48 = lns[:, b0:b0 + 48]
                    mv = lns[:, b0 + 48:b0 + 50]
                    mean = lns[:, b0 + 48:b0 + 49]
                    var = lns[:, b0 + 49:b0 + 50]
                    sd = lns[:, b0 + 50:b0 + 51]
                    tmp = lns[:, b0 + 51:b0 + 52]
                    for q in range(8):
                        n = di(vector.bn_stats(lns[:, b0 + 6 * q:b0 + 6 * q + 6],
                                               xbuf[:, 512 * q:512 * (q + 1)]))
                    dw(n)
                    n = di(vector.bn_aggr(mv, bn48))
                    dw(n)
                    dc(vector.tensor_scalar(out=var, in0=var,
                                             scalar1=float(EPS),
                                             scalar2=None, op0=OP.add), vVAR)
                    vector.wait_ge(aRS, n_aRS(c, m))
                    n = di(vector.reciprocal(tmp, sd))
                    dw(n)
                    n = di(vector.tensor_tensor(out=sd, in0=tmp, in1=tmp,
                                                op=OP.mult))
                    dw(n)
                    n = di(vector.tensor_tensor(out=sd, in0=sd, in1=var,
                                                op=OP.mult))
                    dw(n)
                    n = di(vector.tensor_scalar(out=sd, in0=sd, scalar1=-0.5,
                                                scalar2=1.5, op0=OP.mult,
                                                op1=OP.add))
                    dw(n)
                    n = di(vector.tensor_tensor(out=sd, in0=tmp, in1=sd,
                                                op=OP.mult))
                    dw(n)
                    vector.wait_ge(pLT, c * 128 + m * KT)
                    dc(vector.tensor_scalar(out=xn16[:, :], in0=xbuf,
                                             scalar1=mean, scalar2=sd,
                                             op0=OP.subtract, op1=OP.mult),
                       vXN)
                vector.wait_ge(pGH, c * 2 + 1)
                for m in range(NT):
                    dc(vector.tensor_copy(g9f[:, m * 9:(m + 1) * 9],
                                          quad[m][:, 0:9]), vG9)
                vector.wait_ge(pGH, c * 2 + 2)
                for m in range(NT):
                    dc(vector.tensor_copy(h16[:, m * 128:(m + 1) * 128],
                                          quad[m][:, 0:128]), vG9)
                vector.wait_ge(vG9, c * 8 + 8)
                for m in range(NT):
                    n = di(vector.reduce_max(mx4[:, m:m + 1],
                                             g9f[:, m * 9:m * 9 + 8],
                                             axis=mybir.AxisListType.X))
                dw(n)
                dc(vector.tensor_scalar(out=nmx4[:, :], in0=mx4[:, :],
                                        scalar1=-1.0, scalar2=None,
                                        op0=OP.mult), vNMX)
                vector.wait_ge(aEXP, c * 6 + 6)
                n = di(vector.reciprocal(rse4[:, :], se4[:, :]))
                n = di(vector.tensor_scalar(out=thv4[:, :], in0=th4[:, :], scalar1=0.0625,
                                            scalar2=0.0625, op0=OP.mult,
                                            op1=OP.add))
                dw(n)
                for m in range(NT):
                    n = di(vector.tensor_scalar(
                        out=w8[:, m * 8:(m + 1) * 8],
                        in0=e8[:, m * 8:(m + 1) * 8],
                        scalar1=rse4[:, m:m + 1], scalar2=None, op0=OP.mult))
                dw(n)
                for m in range(NT):
                    n = di(vector.tensor_scalar(
                        out=w8[:, m * 8:(m + 1) * 8],
                        in0=w8[:, m * 8:(m + 1) * 8],
                        scalar1=thv4[:, m:m + 1], scalar2=0.0,
                        op0=OP.subtract, op1=OP.max))
                dw(n)
                for m in range(NT):
                    n = di(vector.reduce_sum(ws4[:, m:m + 1],
                                             w8[:, m * 8:(m + 1) * 8],
                                             axis=mybir.AxisListType.X))
                dw(n)
                n = di(vector.tensor_scalar(out=wz4[:, :], in0=ws4[:, :], scalar1=0.0,
                                            scalar2=None, op0=OP.is_equal))
                dw(n)
                n = di(vector.tensor_tensor(out=ws4[:, :], in0=ws4[:, :], in1=wz4[:, :],
                                            op=OP.add))
                dw(n)
                n = di(vector.reciprocal(wr2[:, :], ws4[:, :]))
                dw(n)
                n = di(vector.tensor_scalar(out=wr2[:, :], in0=wr2[:, :], scalar1=2.0,
                                            scalar2=None, op0=OP.mult))
                dw(n)
                for m in range(NT):
                    n = di(vector.tensor_scalar(
                        out=wn[:, m * 8:(m + 1) * 8],
                        in0=w8[:, m * 8:(m + 1) * 8],
                        scalar1=wr2[:, m:m + 1], scalar2=None, op0=OP.mult))
                dw(n)
                for m in range(NT):
                    for e in range(8):
                        ins = vector.tensor_scalar(
                            out=h16[:, m * 128 + e * 16:m * 128 + (e + 1) * 16],
                            in0=h16[:, m * 128 + e * 16:m * 128 + (e + 1) * 16],
                            scalar1=wn[:, m * 8 + e:m * 8 + e + 1],
                            scalar2=None, op0=OP.mult)
                        if m == NT - 1 and e == 7:
                            dc(ins, vGA)
                        else:
                            di(ins)
                for m in range(NT):
                    lo = st[:, 3 * m:3 * m + 1]
                    wdh = st[:, 3 * m + 1:3 * m + 2]
                    mid = st[:, 3 * m + 2:3 * m + 3]
                    predf = cu[:, :]
                    vector.wait_ge(aGE, n_aGE(c, NW - 1, m))
                    n = di(vector.memset(lo, 0.0))
                    n = di(vector.reduce_max(mid, S[:, m * I:(m + 1) * I],
                                             axis=mybir.AxisListType.X,
                                             apply_absolute_value=True))
                    dw(n)
                    n = di(vector.tensor_scalar(out=wdh, in0=mid, scalar1=0.5,
                                                scalar2=None, op0=OP.mult))
                    dw(n)
                    dc(vector.tensor_copy(mid, wdh), vMID)
                    for it in range(ITERS):
                        vector.wait_ge(vMID, n_vMID(c, m, it))
                        if it == 0:
                            vector.wait_ge(aCR, c * NT * ITERS + m * ITERS + 1)
                        ins_cnt = vector.tensor_scalar(
                            out=junkL, in0=sab[:, 0:SEL_L],
                            scalar1=mid, scalar2=0.0,
                            op0=OP.is_le, op1=OP.add,
                            accum_out=cl[:, 0:1])
                        ins_cnt.annotate("selL")
                        n = di(ins_cnt)
                        vector.wait_ge(aCR, n_aCR(c, m, it))
                        dw(n)
                        n = di(vector.scalar_tensor_tensor(
                            out=predf, in0=cl[:, 1:2], scalar=0.5,
                            in1=cl[:, 0:1], op0=OP.mult, op1=OP.add))
                        dw(n)
                        n = di(vector.tensor_scalar(
                            out=predf, in0=predf, scalar1=K_EFF,
                            scalar2=None, op0=OP.is_lt))
                        dw(n)
                        n = di(vector.scalar_tensor_tensor(
                            out=lo, in0=predf, scalar=wdh, in1=lo,
                            op0=OP.mult, op1=OP.add))
                        n = di(vector.tensor_scalar(
                            out=wdh, in0=wdh, scalar1=0.5, scalar2=None,
                            op0=OP.mult))
                        dw(n)
                        dc(vector.tensor_add(mid, lo, wdh), vMID)
                    vector.wait_ge(vMID, n_vMID(c, m, ITERS))
                    n = di(vector.tensor_scalar(
                        out=wdh, in0=wdh, scalar1=2.0, scalar2=None,
                        op0=OP.mult))
                    dw(n)
                    n = di(vector.tensor_add(mid, lo, wdh))
                    dw(n)
                    hi = mid
                    n = di(vector.tensor_scalar(
                        out=junkL, in0=sab[:, 0:SEL_L], scalar1=hi,
                        scalar2=None, op0=OP.is_gt))
                    n = di(vector.tensor_scalar(
                        out=junkR, in0=sab[:, SEL_L:I], scalar1=hi,
                        scalar2=None, op0=OP.is_gt))
                    dw(n)
                    di(vector.tensor_tensor(
                        out=S[:, m * I:m * I + SEL_L], in0=junkL,
                        in1=S[:, m * I:m * I + SEL_L], op=OP.mult))
                    dc(vector.tensor_tensor(
                        out=S[:, m * I + SEL_L:(m + 1) * I], in0=junkR,
                        in1=S[:, m * I + SEL_L:(m + 1) * I], op=OP.mult), vMK)
                for wi, (w, wl) in enumerate(WINS):
                    for m in range(NT):
                        g = wi * NT + m
                        if wi == 0:
                            vector.wait_ge(vMK, n_vMK(c, m))
                        vector.wait_ge(pU, n_pU(c, wi, m))
                        dc(vector.tensor_tensor(
                            out=S[:, m * I + w:m * I + w + wl],
                            in0=S[:, m * I + w:m * I + w + wl],
                            in1=quad[g % 6][:, 0:wl], op=OP.mult), vPC)
                for hw in range(NWV):
                    for m in range(NT):
                        j = hw * NT + m
                        p = j % 2
                        vector.wait_ge(pD, n_pD(c, hw, m))
                        vector.wait_ge(dFR[p], 16 * (c * 16 + j // 2 + 1))
                        vector.wait_ge(dXS[p], 16 * (c * 16 + j // 2 + 1))
                        iu = c * 16 + j // 2
                        if iu > 0:
                            vector.wait_ge(dY[p], 16 * iu)
                        n = di(vector.tensor_tensor(
                            out=out32[p], in0=quad[(hw * NT + m) % 6][:, 0:512],
                            in1=ffst[:, (2 + p) * 512:(3 + p) * 512],
                            op=OP.add))
                        dw(n)
                        dc(vector.tensor_tensor(
                            out=out32[p], in0=out32[p],
                            in1=xs[:, p * 512:(p + 1) * 512], op=OP.add), vEP)

    return nc


def kernel(L0_attn_x, L0_attn_w_gate, L0_attn_w_up, L0_attn_w_down,
           L0_ffn_W_base, L0_ffn_b_base, L0_ffn_W_gate, L0_ffn_W_thr,
           L0_ffn_b_thr, L0_ffn_A_w, L0_ffn_B_w):
    x = np.ascontiguousarray(np.asarray(L0_attn_x, dtype=np.float32)
                             ).reshape(NCORE * TOK, H)
    wgT = np.ascontiguousarray(np.asarray(L0_attn_w_gate, np.float32).T)
    wuT = np.ascontiguousarray(np.asarray(L0_attn_w_up, np.float32).T)
    wdT = np.ascontiguousarray(np.asarray(L0_attn_w_down, np.float32).T)
    wbT = np.ascontiguousarray(np.asarray(L0_ffn_W_base, np.float32).T)
    bb = np.ascontiguousarray(np.asarray(L0_ffn_b_base, np.float32)
                              ).reshape(1, O)
    wz = np.asarray(L0_ffn_W_gate, np.float32).T
    wthr = np.asarray(L0_ffn_W_thr, np.float32).reshape(H, 1)
    wzthrT = np.ascontiguousarray(np.concatenate([wz, wthr], axis=1))
    bthr = np.broadcast_to(np.asarray(L0_ffn_b_thr, np.float32).reshape(1, 1),
                           (128, 1)).copy()
    awT = np.ascontiguousarray(
        np.asarray(L0_ffn_A_w, np.float32).reshape(128, H).T)
    bwT = np.ascontiguousarray(
        np.asarray(L0_ffn_B_w, np.float32).transpose(0, 2, 1).reshape(128, O))
    eye = np.eye(128, dtype=np.float32)

    nc = build_program()
    shared = dict(wgT=wgT, wuT=wuT, wdT=wdT, wbT=wbT, bb=bb, wzthrT=wzthrT,
                  bthr=bthr, awT=awT, bwT=bwT, eye=eye)
    in_maps = []
    for cid in range(NCORE):
        m = dict(shared)
        m["x"] = np.ascontiguousarray(x[cid * TOK:(cid + 1) * TOK])
        in_maps.append(m)
    res = run_bass_kernel_spmd(nc, in_maps, core_ids=list(range(NCORE)))
    out = np.concatenate([r["y"] for r in res.results], axis=0)
    return out.reshape(2, 4096, 4096)
